# version 06
# baseline (speedup 1.0000x reference)
import numpy as np
import ml_dtypes
import concourse.bass as bass
import concourse.mybir as mybir
from concourse.bass_utils import run_bass_kernel_spmd

F32 = mybir.dt.float32
BF16 = mybir.dt.bfloat16
AF = mybir.ActivationFunctionType
ALU = mybir.AluOpType

EPS = 1e-5

FULL_CFG = dict(
    D=1024,
    C=512,
    NCH=4,
    n_cores=8,
    pairs=[[0, 1], [2, 3], [4, 5], [6, 7]],
    D2=512,
    KC=4,
)


class Arena:

    def __init__(self, tensor, words):
        self.t = tensor
        self.words = words

    def f32(self, off, n):
        assert off + n <= self.words, (off, n, self.words)
        return self.t[:, off:off + n]

    def bf16(self, off, nwords):
        assert off + nwords <= self.words
        return self.t[:, off:off + nwords].bitcast(BF16)


class Sched:

    ENGS = ("sync", "scalar", "vector", "tensor", "gpsimd")

    def __init__(self):
        self.ops = {e: [] for e in self.ENGS}
        self.cnt = {}

    def val(self, sem):
        return self.cnt.get(id(sem), 0)

    def op(self, eng, fn, waits=(), inc=None, inc_n=1):
        waits = [(s, v) for (s, v) in waits if v]
        newv = None
        if inc is not None:
            self.cnt[id(inc)] = self.cnt.get(id(inc), 0) + inc_n
            newv = self.cnt[id(inc)]

        def emit(h, _waits=waits, _fn=fn, _inc=inc, _inc_n=inc_n):
            for s, v in _waits:
                h.wait_ge(s, v)
            instr = _fn(h)
            if _inc is not None:
                instr.then_inc(_inc, _inc_n)

        self.ops[eng].append(emit)
        return newv

    def raw(self, eng, fn):
        self.ops[eng].append(fn)


def build_program(cfg):
    D = cfg["D"]; C = cfg["C"]; NCH = cfg["NCH"]
    D2 = cfg["D2"]; KC = cfg["KC"]
    KT = D // 128
    MT = D // 128
    T2 = D2 // 128
    TCORE = C * NCH
    EXP_SCALE = 1.0 / float(np.sqrt(D))
    RD = 1.0 / D
    PSW = min(512, max(D, 128))

    nc = bass.Bass("TRN2", target_bir_lowering=False)

    xT = nc.dram_tensor("xT", [D, TCORE], F32, kind="ExternalInput").ap()
    wkT = nc.dram_tensor("wkT", [D, D], F32, kind="ExternalInput").ap()
    wkaT = nc.dram_tensor("wkaT", [D, D], F32, kind="ExternalInput").ap()
    wvT = nc.dram_tensor("wvT", [D, D], F32, kind="ExternalInput").ap()
    selv = nc.dram_tensor("selv", [128, 2], F32, kind="ExternalInput").ap()
    gammaC = nc.dram_tensor("gammaC", [128, KT], F32, kind="ExternalInput").ap()
    betaC = nc.dram_tensor("betaC", [128, KT], F32, kind="ExternalInput").ap()
    bkC = nc.dram_tensor("bkC", [128, MT], F32, kind="ExternalInput").ap()
    bkaC = nc.dram_tensor("bkaC", [128, MT], F32, kind="ExternalInput").ap()
    bvC = nc.dram_tensor("bvC", [128, MT], F32, kind="ExternalInput").ap()
    cfC = nc.dram_tensor("cfC", [128, MT], F32, kind="ExternalInput").ap()
    BB = D2 // 16
    atgR = nc.dram_tensor("atgR", [KC, D2, BB], F32, kind="ExternalInput").ap()
    atgI = nc.dram_tensor("atgI", [KC, D2, BB], F32, kind="ExternalInput").ap()
    btgR = nc.dram_tensor("btgR", [KC, 128, 16], F32, kind="ExternalInput").ap()
    btgI = nc.dram_tensor("btgI", [KC, 128, 16], F32, kind="ExternalInput").ap()
    outT = nc.dram_tensor("outT", [D, TCORE], F32, kind="ExternalOutput").ap()
    sc_kk = nc.dram_tensor("sc_kk", [NCH, D, C], F32).ap()
    sc_den = nc.dram_tensor("sc_den", [NCH, D, C], F32).ap()
    cc_in = nc.dram_tensor("cc_in", [128, 2 * MT], F32)
    cc_out = nc.dram_tensor("cc_out", [128, 2 * MT], F32)

    sch = Sched()
    CW = C

    A_KEYS = 0
    A_KWKK = 8 * CW
    A_DEN = 16 * CW
    A_KEYSB = 24 * CW
    A_W0 = 24 * CW + 4 * CW
    A_XST = A_W0 + 2 * CW
    A_XSQ = 32 * CW
    A_SBC = 34 * CW
    B_XRES = 32 * CW
    B_SBC = 34 * CW
    B_S1BC = 35 * CW
    B_WST = 36 * CW
    AR_WORDS = B_WST + max(2 * D + D // 2, 5 * D2)

    import contextlib
    with contextlib.ExitStack() as _st:
        xB_t = _st.enter_context(nc.sbuf_tensor("xB", [128, KT * TCORE], BF16))
        Wbuf0_t = _st.enter_context(nc.sbuf_tensor("Wbuf0", [128, KT * D], BF16))
        Wbuf1_t = _st.enter_context(nc.sbuf_tensor("Wbuf1", [128, KT * D], BF16))
        selt = _st.enter_context(nc.sbuf_tensor("selt", [128, 2], F32))
        gammaT = _st.enter_context(nc.sbuf_tensor("gammaT", [128, KT], F32))
        betaT = _st.enter_context(nc.sbuf_tensor("betaT", [128, KT], F32))
        betaB = _st.enter_context(nc.sbuf_tensor("betaB", [128, KT], BF16))
        bkT = _st.enter_context(nc.sbuf_tensor("bkT", [128, MT], F32))
        bkaT = _st.enter_context(nc.sbuf_tensor("bkaT", [128, MT], F32))
        bvT = _st.enter_context(nc.sbuf_tensor("bvT", [128, MT], F32))
        cfT = _st.enter_context(nc.sbuf_tensor("cfT", [128, MT], F32))
        bka32T = _st.enter_context(nc.sbuf_tensor("bka32T", [128, MT], F32))
        ckT = _st.enter_context(nc.sbuf_tensor("ckT", [128, MT], F32))
        cvT = _st.enter_context(nc.sbuf_tensor("cvT", [128, MT], F32))
        vkB = _st.enter_context(nc.sbuf_tensor("vkB", [1, D], BF16))
        vvB = _st.enter_context(nc.sbuf_tensor("vvB", [1, D], BF16))
        vfB = _st.enter_context(nc.sbuf_tensor("vfB", [1, D], BF16))
        s_row = _st.enter_context(nc.sbuf_tensor("s_row", [1, TCORE], F32))
        s1_row = _st.enter_context(nc.sbuf_tensor("s1_row", [1, TCORE], F32))
        smB_row = _st.enter_context(nc.sbuf_tensor("smB_row", [1, TCORE], BF16))
        s1mB_row = _st.enter_context(nc.sbuf_tensor("s1mB_row", [1, TCORE], BF16))
        onesB = _st.enter_context(nc.sbuf_tensor("onesB", [128, 1], BF16))
        onesF = _st.enter_context(nc.sbuf_tensor("onesF", [1, 128], F32))
        onesC = _st.enter_context(nc.sbuf_tensor("onesC", [128, C], F32))
        epsT = _st.enter_context(nc.sbuf_tensor("epsT", [128, 1], F32))
        atg_s = _st.enter_context(nc.sbuf_tensor("atg_s", [128, 2 * KC * T2 * BB], F32))
        btg_s = _st.enter_context(nc.sbuf_tensor("btg_s", [128, 2 * KC * 16], F32))
        denS = _st.enter_context(nc.sbuf_tensor("denS", [128, MT], F32))
        pkS = _st.enter_context(nc.sbuf_tensor("pkS", [128, MT], F32))
        pk0p = _st.enter_context(nc.sbuf_tensor("pk0p", [128, NCH * MT], F32))
        pk0tot = _st.enter_context(nc.sbuf_tensor("pk0tot", [128, MT], F32))
        contrib = _st.enter_context(nc.sbuf_tensor("contrib", [128, 2 * MT], F32))
        carryraw = _st.enter_context(nc.sbuf_tensor("carryraw", [128, 2 * MT], F32))
        cD = _st.enter_context(nc.sbuf_tensor("cD", [128, MT], F32))
        cP = _st.enter_context(nc.sbuf_tensor("cP", [128, MT], F32))
        stw = _st.enter_context(nc.sbuf_tensor("stw", [1, 5 * C], F32))
        arena_t = _st.enter_context(nc.sbuf_tensor("arena", [128, AR_WORDS], F32))
        pool0 = _st.enter_context(nc.psum_tensor("pool0", [128, C], F32))
        pool1 = _st.enter_context(nc.psum_tensor("pool1", [128, C], F32))
        pool2 = _st.enter_context(nc.psum_tensor("pool2", [128, C], F32))
        pool3 = _st.enter_context(nc.psum_tensor("pool3", [128, C], F32))
        ps_sx = _st.enter_context(nc.psum_tensor("ps_sx", [1, C], F32))
        ps_sxx = _st.enter_context(nc.psum_tensor("ps_sxx", [1, C], F32))
        ps_bc = _st.enter_context(nc.psum_tensor("ps_bc", [128, C], F32))
        ps_misc = _st.enter_context(nc.psum_tensor("ps_misc", [128, PSW], F32))
        semLXa = _st.enter_context(nc.semaphore("semLXa"))
        semLXb = _st.enter_context(nc.semaphore("semLXb"))
        semLXc = _st.enter_context(nc.semaphore("semLXc"))
        semLXd = _st.enter_context(nc.semaphore("semLXd"))
        semLV = _st.enter_context(nc.semaphore("semLV"))
        semKL = _st.enter_context(nc.semaphore("semKL"))
        semLWa = _st.enter_context(nc.semaphore("semLWa"))
        semLWb = _st.enter_context(nc.semaphore("semLWb"))
        semLW = _st.enter_context(nc.semaphore("semLW"))
        semRK = _st.enter_context(nc.semaphore("semRK"))
        semRD = _st.enter_context(nc.semaphore("semRD"))
        semRXa = _st.enter_context(nc.semaphore("semRXa"))
        semRXb = _st.enter_context(nc.semaphore("semRXb"))
        semG = _st.enter_context(nc.semaphore("semG"))
        semSPD = _st.enter_context(nc.semaphore("semSPD"))
        semSPK = _st.enter_context(nc.semaphore("semSPK"))
        semOUT = _st.enter_context(nc.semaphore("semOUT"))
        semCC = _st.enter_context(nc.semaphore("semCC"))
        semCCD = _st.enter_context(nc.semaphore("semCCD"))
        semT = _st.enter_context(nc.semaphore("semT"))
        semV = _st.enter_context(nc.semaphore("semV"))
        semS = _st.enter_context(nc.semaphore("semS"))
        ar = Arena(arena_t, AR_WORDS)
        pool = [pool0, pool1, pool2, pool3]

        def xB(k):
            return xB_t[:, k * TCORE:(k + 1) * TCORE]

        def W0(k):
            return Wbuf0_t[:, k * D:(k + 1) * D]

        def W1(k):
            return Wbuf1_t[:, k * D:(k + 1) * D]

        keysT = [ar.f32(A_KEYS + t * CW, CW) for t in range(MT)]
        kwkk = [ar.f32(A_KWKK + t * CW, CW) for t in range(MT)]
        denomL = [ar.f32(A_DEN + t * CW, CW) for t in range(MT)]
        keysB = [ar.bf16(A_KEYSB + t * CW // 2, CW // 2) for t in range(MT)]
        w0tmp = [ar.f32(A_W0 + i * CW, CW) for i in range(2)]
        xstage = [ar.f32(A_XST, CW), ar.f32(A_XST + CW, CW),
                  ar.f32(33 * CW, CW), ar.f32(35 * CW, CW)]
        xsqB = [ar.bf16(A_XSQ + i * CW // 2, CW // 2) for i in range(2)]
        sbcA = ar.f32(A_SBC, CW)
        wstageA = [ar.f32(A_KEYS, D), ar.f32(A_KEYS + D, D)]
        rawB_A = ar.bf16(A_KEYS + 2 * D, D // 2)
        kkB = [ar.f32(A_KEYS + t * CW, CW) for t in range(MT)]
        denR = [ar.f32(A_KWKK + t * CW, CW) for t in range(MT)]
        pkbuf = [ar.f32(A_DEN + t * CW, CW) for t in range(MT)]
        vbuf = [ar.f32(A_KEYSB + t * CW, CW) for t in range(MT)]
        xres = [ar.f32(B_XRES + i * CW, CW) for i in range(2)]
        sbcB = ar.f32(B_SBC, CW)
        s1bcB = ar.f32(B_S1BC, CW)
        wstageB = [ar.f32(B_WST, D), ar.f32(B_WST + D, D)]
        rawB_B = ar.bf16(B_WST + 2 * D, D // 2)
        KRON_IN_XSTAGE = CW >= D2
        if KRON_IN_XSTAGE:
            kronACC = [ar.f32(A_XST, D2), ar.f32(A_XST + CW, D2)]
            kronTMP = [ar.f32(33 * CW, D2), ar.f32(35 * CW, D2)]
        else:
            kronACC = [ar.f32(B_WST, D2), ar.f32(B_WST + D2, D2)]
            kronTMP = [ar.f32(B_WST + 2 * D2, D2), ar.f32(B_WST + 3 * D2, D2)]
            assert 4 * D2 <= max(2 * D + D // 2, 5 * D2)

        def ws(i):
            return stw[0:1, i * C:(i + 1) * C]

        sch.op("vector", lambda h: h.memset(onesC[:], 1.0), inc=semV)
        sch.op("vector", lambda h: h.memset(onesB[:], 1.0), inc=semV)
        sch.op("vector", lambda h: h.memset(onesF[:], 1.0), inc=semV)
        sch.op("vector", lambda h: h.memset(epsT[:], EPS), inc=semV)
        sch.op("vector", lambda h: h.memset(denS[:], 0.0), inc=semV)
        sch.op("vector", lambda h: h.memset(pkS[:], 0.0), inc=semV)
        V_const = sch.val(semV)

        def dma(eng, dst, src, sem, waits=()):
            return sch.op(eng, lambda h, d=dst, s=src: h.dma_start(out=d, in_=s),
                          waits=waits, inc=sem, inc_n=16)

        lw0 = dma("sync", selt[:, :], selv[:, :], semLV)
        dma("sync", gammaT[:, :], gammaC[:, :], semLV)
        dma("sync", betaT[:, :], betaC[:, :], semLV)
        dma("sync", bkT[:, :], bkC[:, :], semLV)
        dma("sync", bkaT[:, :], bkaC[:, :], semLV)
        dma("sync", bvT[:, :], bvC[:, :], semLV)
        lw_vecs = dma("sync", cfT[:, :], cfC[:, :], semLV)
        lw0 = lw_vecs
        semLX = [semLXa, semLXb, semLXc, semLXd]
        LXc = {}

        def emit_xload(gi, waits=()):
            j, t = divmod(gi, KT)
            LXc[(j, t)] = sch.op("gpsimd", lambda h, j_=j, t_=t, g_=gi: h.dma_start(
                out=xstage[g_ % 4][:, :], in_=xT[128 * t_:128 * (t_ + 1), C * j_:C * (j_ + 1)]),
                waits=waits, inc=semLX[gi % 4], inc_n=16)

        for gi in range(min(4, NCH * KT)):
            emit_xload(gi)

        S_beta = sch.op("scalar", lambda h: h.copy(out=betaB[:], in_=betaT[:]),
                        waits=[(semLV, lw_vecs)], inc=semS)
        S_bka32 = sch.op("scalar", lambda h: h.mul(out=bka32T[:], in_=bkaT[:], mul=EXP_SCALE),
                         waits=[(semLV, lw_vecs)], inc=semS)

        semLW = [semLWa, semLWb]

        def prep_matrix(matT, Wdst, stage, rawB, fold_gamma, want_cb, cb_out, bias_tile,
                        want_v, vB_out, stage_guard=None, fold_gate=None):
            fold_done = []
            last_ckmm = None
            for k in range(KT):
                waits = list(stage_guard) if stage_guard else []
                if len(fold_done) >= 2:
                    waits.append((semV, fold_done[-2]))
                if last_ckmm is not None:
                    waits.append((semT, last_ckmm))
                ld = sch.op("sync", lambda h, k_=k, w=matT: h.dma_start(
                    out=stage[k_ % 2][:, :], in_=w[128 * k_:128 * (k_ + 1), :]),
                    waits=waits, inc=semLW[k % 2], inc_n=16)
                if want_cb:
                    for m in range(MT):
                        inc = semT if m == MT - 1 else None
                        last_ckmm = sch.op("tensor", lambda h, m_=m, k_=k: h.matmul(
                            ps_misc[:, k_ * MT + m_:k_ * MT + m_ + 1],
                            stage[k_ % 2][:, 128 * m_:128 * (m_ + 1)],
                            betaT[:, k_:k_ + 1], start=True, stop=True),
                            waits=[(semLW[k % 2], ld), (semLV, lw_vecs)], inc=inc) or last_ckmm
                fg = list(fold_gate) if fold_gate else []
                if fold_gamma:
                    fv = sch.op("vector", lambda h, k_=k: h.tensor_scalar(
                        Wdst(k_)[:, :], stage[k_ % 2][:, :], gammaT[:, k_:k_ + 1], None, ALU.mult),
                        waits=[(semLW[k % 2], ld), (semLV, lw_vecs)] + fg, inc=semV)
                else:
                    fv = sch.op("vector", lambda h, k_=k: h.tensor_copy(
                        Wdst(k_)[:, :], stage[k_ % 2][:, :]),
                        waits=[(semLW[k % 2], ld)] + fg, inc=semV)
                fold_done.append(fv)
            ck_psum = sch.val(semT)
            cb_v = None
            if want_cb:
                cb_v = sch.op("vector", lambda h: h.tensor_add(cb_out[:, :], ps_misc[:, 0:MT], bias_tile[:, :]),
                              waits=[(semT, ck_psum), (semLV, lw_vecs)], inc=semV)
                for k in range(1, KT):
                    cb_v = sch.op("vector", lambda h, k_=k: h.tensor_add(
                        cb_out[:, :], cb_out[:, :], ps_misc[:, k_ * MT:(k_ + 1) * MT]),
                        waits=[(semV, cb_v)], inc=semV)
            vlast = None
            if want_v:
                nhalf = (D + PSW - 1) // PSW
                for hh in range(nhalf):
                    w = min(PSW, D - PSW * hh)
                    for k in range(KT):
                        waits = [(semV, V_const), (semV, fold_done[k])]
                        if vlast is not None:
                            waits.append((semS, vlast))
                        elif want_cb:
                            waits.append((semV, cb_v))
                        sch.op("tensor", lambda h, k_=k, hh_=hh, w_=w: h.matmul(
                            ps_misc[0:1, 0:w_], onesB[:, :], Wdst(k_)[:, PSW * hh_:PSW * hh_ + w_],
                            start=(k_ == 0), stop=(k_ == KT - 1)),
                            waits=waits, inc=semT if k == KT - 1 else None)
                    tps = sch.val(semT)
                    vlast = sch.op("scalar", lambda h, hh_=hh, w_=w: h.copy(
                        out=vB_out[0:1, PSW * hh_:PSW * hh_ + w_], in_=ps_misc[0:1, 0:w_]),
                        waits=[(semT, tps)], inc=semS)
            return fold_done, cb_v, vlast

        wk_fold, ck_v, vk_s = prep_matrix(wkT, W0, wstageA, rawB_A, True, True, ckT, bkT, True, vkB)
        WK_END = [(semV, sch.val(semV)), (semS, sch.val(semS)), (semT, sch.val(semT))]
        wka_fold, _, _ = prep_matrix(wkaT, W1, wstageA, rawB_A, False, False, None, None, False, None,
                                     stage_guard=WK_END)
        for half, (ag, bg) in enumerate(((atgR, btgR), (atgI, btgI))):
            for c in range(KC):
                for t in range(T2):
                    off = ((half * KC + c) * T2 + t) * BB
                    dma("sync", atg_s[:, off:off + BB], ag[c, 128 * t:128 * (t + 1), :], semKL)
                boff = (half * KC + c) * 16
                dma("sync", btg_s[:, boff:boff + 16], bg[c, :, :], semKL)
        KRON_LOADED = sch.val(semKL)
        WK_READY = [(semV, wk_fold[k]) for k in range(KT)]
        WKA_READY = [(semV, wka_fold[k]) for k in range(KT)]
        VK_READY = (semS, vk_s)
        CK_READY = (semV, ck_v)
        A_SPACE_GATE = [(semV, sch.val(semV)), (semS, sch.val(semS)), (semT, sch.val(semT))]

        XBc = {}; ke_evac = {}; KEYv = {}; KKv = {}; W0v = {}
        SPK = {}; SPD = {}; DENSv = {}; kw_s = {}
        sbc_ev = {}; stats_done = {}; sq = {}
        SPA_done = {}
        stats_mfree = {}
        SROWV = {}; SROWS = {}
        kaps_last = {}

        def emit_casts(j):
            for t in range(KT):
                gi0 = j * KT + t
                XBc[(j, t)] = sch.op("scalar", lambda h, j_=j, t_=t, g_=gi0: h.copy(
                    out=xB(t_)[:, C * j_:C * (j_ + 1)], in_=xstage[g_ % 4][:, :]),
                    waits=[(semLX[gi0 % 4], LXc[(j, t)])], inc=semS)
                gi = gi0 + 4
                if gi < NCH * KT:
                    emit_xload(gi, waits=[(semS, XBc[(j, t)])])

        def emit_stats(j):
            for t in range(KT):
                gi = j * KT + t
                waits = [(semS, XBc[(j, t)])]
                if gi >= 2:
                    waits.append((semT, sq[("mm", gi - 2)]))
                sq[(j, t)] = sch.op("scalar", lambda h, j_=j, t_=t: h.activation(
                    out=xsqB[t_ % 2][:, :], in_=xB(t_)[:, C * j_:C * (j_ + 1)], func=AF.Square),
                    waits=waits, inc=semS)
                sch.op("tensor", lambda h, j_=j, t_=t: h.matmul(
                    ps_sx[0:1, :], onesB[:, :], xB(t_)[:, C * j_:C * (j_ + 1)],
                    start=(t_ == 0), stop=(t_ == KT - 1)),
                    waits=[(semS, XBc[(j, t)]), (semV, V_const)] +
                          ([(semV, stats_mfree[j - 1])] if j > 0 else []))
                sq[("mm", gi)] = sch.op("tensor", lambda h, j_=j, t_=t: h.matmul(
                    ps_sxx[0:1, :], onesB[:, :], xsqB[t_ % 2][:, :],
                    start=(t_ == 0), stop=(t_ == KT - 1)),
                    waits=[(semS, sq[(j, t)])], inc=semT)
            STATS_PS = sch.val(semT)
            v_m = sch.op("vector", lambda h: h.tensor_scalar_mul(ws(0), ps_sx[0:1, :], RD),
                         waits=[(semT, STATS_PS)] +
                               ([(semV, stats_done[j - 1])] if j > 0 else []),
                         inc=semV)
            v_msq = sch.op("vector", lambda h: h.tensor_scalar_mul(ws(1), ps_sxx[0:1, :], RD),
                           waits=[(semT, STATS_PS)], inc=semV)
            stats_mfree[j] = v_msq
            v_m2 = sch.op("vector", lambda h: h.tensor_mul(ws(2), ws(0), ws(0)),
                          waits=[(semV, v_m)], inc=semV)
            v_var = sch.op("vector", lambda h: h.tensor_sub(ws(3), ws(1), ws(2)),
                           waits=[(semV, v_msq), (semV, v_m2)], inc=semV)
            s_sq1 = sch.op("scalar", lambda h: h.activation(out=ws(4), in_=ws(3), func=AF.Sqrt, bias=epsT[0:1, 0:1]),
                           waits=[(semV, v_var)], inc=semS)
            v_s1 = sch.op("vector", lambda h, j_=j: h.reciprocal(
                out=s1_row[0:1, C * j_:C * (j_ + 1)], in_=ws(4)),
                waits=[(semS, s_sq1)], inc=semV)
            v_a = sch.op("vector", lambda h, j_=j: h.tensor_mul(
                ws(2), s1_row[0:1, C * j_:C * (j_ + 1)], s1_row[0:1, C * j_:C * (j_ + 1)]),
                waits=[(semV, v_s1)], inc=semV)
            v_var2 = sch.op("vector", lambda h: h.tensor_mul(ws(1), ws(3), ws(2)),
                            waits=[(semV, v_a)], inc=semV)
            s_sq2 = sch.op("scalar", lambda h: h.activation(out=ws(4), in_=ws(1), func=AF.Sqrt, bias=epsT[0:1, 0:1]),
                           waits=[(semV, v_var2), (semV, v_s1)], inc=semS)
            v_s2 = sch.op("vector", lambda h: h.reciprocal(out=ws(3), in_=ws(4)),
                          waits=[(semS, s_sq2)], inc=semV)
            v_s = sch.op("vector", lambda h, j_=j: h.tensor_mul(
                s_row[0:1, C * j_:C * (j_ + 1)], s1_row[0:1, C * j_:C * (j_ + 1)], ws(3)),
                waits=[(semV, v_s2)], inc=semV)
            v_sm = sch.op("vector", lambda h, j_=j: h.tensor_mul(
                ws(2), s_row[0:1, C * j_:C * (j_ + 1)], ws(0)),
                waits=[(semV, v_s)], inc=semV)
            v_smB = sch.op("vector", lambda h, j_=j: h.tensor_scalar(
                smB_row[0:1, C * j_:C * (j_ + 1)], ws(2), -1.0, None, ALU.mult),
                waits=[(semV, v_sm)], inc=semV)
            v_s1m = sch.op("vector", lambda h, j_=j: h.tensor_mul(
                ws(4), s1_row[0:1, C * j_:C * (j_ + 1)], ws(0)),
                waits=[(semV, v_sm), (semV, v_s2)], inc=semV)
            v_s1mB = sch.op("vector", lambda h, j_=j: h.tensor_scalar(
                s1mB_row[0:1, C * j_:C * (j_ + 1)], ws(4), -1.0, None, ALU.mult),
                waits=[(semV, v_s1m)], inc=semV)
            stats_done[j] = v_s1mB
            SROWV[j] = v_s
            SROWS[j] = v_smB

        emit_casts(0)
        if NCH > 1:
            emit_casts(1)
        emit_stats(0)

        def emit_scan_tile(j, t):
            waits = [(semS, kw_s[(j, t)]), (semV, V_const)]
            if j > 0:
                waits += [(semV, DENSv[(j - 1, t)]), (semV, W0v[(j - 1, t)]),
                          (semSPD, SPA_done[j - 1][0]), (semSPK, SPA_done[j - 1][1])]
            sc1 = sch.op("vector", lambda h, t_=t: h.tensor_tensor_scan(
                denomL[t_][:, :], onesC[:, :], kwkk[t_][:, :], denS[:, t_:t_ + 1],
                ALU.mult, ALU.add),
                waits=waits, inc=semV)
            DENSv[(j, t)] = sch.op("vector", lambda h, t_=t: h.tensor_copy(
                denS[:, t_:t_ + 1], denomL[t_][:, C - 1:C]),
                waits=[(semV, sc1)], inc=semV)
            SPD[(j, t)] = sch.op("gpsimd", lambda h, j_=j, t_=t: h.dma_start(
                out=sc_den[j_, 128 * t_:128 * (t_ + 1), :], in_=denomL[t_][:, :]),
                waits=[(semV, sc1)], inc=semSPD, inc_n=16)
            KKv[(j, t)] = sch.op("vector", lambda h, t_=t: h.scalar_tensor_tensor(
                kwkk[t_][:, :], keysT[t_][:, :], ckT[:, t_:t_ + 1], kwkk[t_][:, :],
                ALU.add, ALU.mult),
                waits=[(semV, sc1), (semS, KEYv[(j, t)]), CK_READY], inc=semV)
            rl0 = sch.op("scalar", lambda h, t_=t: h.activation(
                out=denomL[t_][:, :], in_=denomL[t_][:, :], func=AF.Ln),
                waits=[(semSPD, SPD[(j, t)]), (semV, DENSv[(j, t)])], inc=semS)
            rl = sch.op("scalar", lambda h, t_=t: h.activation(
                out=denomL[t_][:, :], in_=denomL[t_][:, :], func=AF.Exp, scale=-1.0),
                waits=[(semS, rl0)], inc=semS)
            waits = [(semV, KKv[(j, t)]), (semS, rl)]
            prev = W0v.get((j, t - 2), W0v.get((j - 1, MT - 2 + t)))
            if prev:
                waits.append((semV, prev))
            W0v[(j, t)] = sch.op("vector", lambda h, j_=j, t_=t: h.scalar_tensor_tensor(
                w0tmp[t_ % 2][:, :], kwkk[t_][:, :], 1.0, denomL[t_][:, :],
                ALU.mult, ALU.mult,
                accum_out=pk0p[:, (j_ * MT + t_):(j_ * MT + t_ + 1)]),
                waits=waits, inc=semV)
            SPK[(j, t)] = sch.op("gpsimd", lambda h, j_=j, t_=t: h.dma_start(
                out=sc_kk[j_, 128 * t_:128 * (t_ + 1), :], in_=kwkk[t_][:, :]),
                waits=[(semV, KKv[(j, t)])], inc=semSPK, inc_n=16)

        for j in range(NCH):
            t_bc = sch.op("tensor", lambda h, j_=j: h.matmul(
                ps_bc[:, :], onesF[0:1, :], s_row[0:1, C * j_:C * (j_ + 1)], start=True, stop=True),
                waits=[(semV, SROWV[j]), (semV, V_const)] +
                      ([(semS, sbc_ev[j - 1])] if j > 0 else []),
                inc=semT)
            sbc_ev[j] = sch.op("scalar", lambda h: h.copy(out=sbcA[:, :], in_=ps_bc[:, :]),
                               waits=[(semT, t_bc)] +
                                     ([(semV, ke_evac[(j - 1, MT - 1)])] if j > 0 else
                                      [(semV, wka_fold[-1])]),
                               inc=semS)

            for t in range(MT):
                if j > 0:
                    emit_scan_tile(j - 1, t)
                m = t
                bank = pool[m % 2]
                for k in range(KT):
                    waits = [WK_READY[k], (semS, XBc[(j, k)])]
                    if k == 0:
                        prev = ke_evac.get((j, m - 2), ke_evac.get((j - 1, MT - 2 + m)))
                        if prev:
                            waits.append((semV, prev))
                    sch.op("tensor", lambda h, j_=j, m_=m, k_=k, b=bank: h.matmul(
                        b[:, :], W0(k_)[:, 128 * m_:128 * (m_ + 1)], xB(k_)[:, C * j_:C * (j_ + 1)],
                        start=(k_ == 0), stop=False),
                        waits=waits)
                KPS = sch.op("tensor", lambda h, j_=j, m_=m, b=bank: h.matmul(
                    b[:, :], vkB[0:1, 128 * m_:128 * (m_ + 1)], smB_row[0:1, C * j_:C * (j_ + 1)],
                    start=False, stop=True),
                    waits=[VK_READY, (semV, SROWS[j])], inc=semT)
                waits = [(semT, KPS), (semS, sbc_ev[j])]
                if j == 0:
                    waits += A_SPACE_GATE
                else:
                    waits.append((semV, KKv[(j - 1, m)]))
                ke_evac[(j, m)] = sch.op("vector", lambda h, m_=m, b=bank: h.tensor_mul(
                    keysT[m_][:, :], b[:, :], sbcA[:, :]),
                    waits=waits, inc=semV)
                waits = [(semV, ke_evac[(j, m)]), CK_READY]
                if j > 0:
                    waits.append((semT, kaps_last[j - 1]))
                KEYv[(j, m)] = sch.op("scalar", lambda h, m_=m: h.activation(
                    out=keysB[m_][:, :], in_=keysT[m_][:, :], func=AF.Identity, bias=ckT[:, m_:m_ + 1]),
                    waits=waits, inc=semS)
            if j == NCH - 1:
                KEYS_A_DONE = sch.val(semT)
            kw_free = sch.val(semSPK)
            kaps = {}
            for f in range(MT):
                bank = pool[2 + f % 2]
                for k in range(KT):
                    waits = [WKA_READY[k], (semS, KEYv[(j, k)])]
                    if k == 0:
                        prev = kw_s.get((j, f - 2), kw_s.get((j - 1, MT - 2 + f)))
                        if prev:
                            waits.append((semS, prev))
                    sch.op("tensor", lambda h, f_=f, k_=k, b=bank: h.matmul(
                        b[:, :], W1(k_)[:, 128 * f_:128 * (f_ + 1)], keysB[k_][:, :],
                        start=(k_ == 0), stop=(k_ == KT - 1)),
                        waits=waits, inc=(semT if k == KT - 1 else None))
                kaps[f] = sch.val(semT)
                waits = [(semT, kaps[f]), (semS, S_bka32)]
                if j > 0:
                    waits.append((semSPK, kw_free))
                kw_s[(j, f)] = sch.op("scalar", lambda h, f_=f: h.activation(
                    out=kwkk[f_][:, :], in_=pool[2 + f_ % 2][:, :], func=AF.Exp,
                    bias=bka32T[:, f_:f_ + 1], scale=EXP_SCALE),
                    waits=waits, inc=semS)
            kaps_last[j] = kaps[MT - 1]
            if j + 2 < NCH:
                emit_casts(j + 2)
            if j + 1 < NCH:
                emit_stats(j + 1)
            if j > 0:
                SPA_done[j - 1] = (sch.val(semSPD), sch.val(semSPK))

        for t in range(MT):
            emit_scan_tile(NCH - 1, t)
        SPA_done[NCH - 1] = (sch.val(semSPD), sch.val(semSPK))

        vsum = None
        for j in range(NCH):
            waits = [(semV, W0v[(NCH - 1, MT - 1)])]
            if vsum:
                waits.append((semV, vsum))
            if j == 0:
                vsum = sch.op("vector", lambda h: h.tensor_copy(pk0tot[:, :], pk0p[:, 0:MT]),
                              waits=waits, inc=semV)
            else:
                vsum = sch.op("vector", lambda h, j_=j: h.tensor_add(
                    pk0tot[:, :], pk0tot[:, :], pk0p[:, j_ * MT:(j_ + 1) * MT]),
                    waits=waits, inc=semV)
        c1 = sch.op("vector", lambda h: h.tensor_scalar(
            contrib[:, 0:MT], denS[:, :], selt[:, 0:1], None, ALU.mult),
            waits=[(semV, DENSv[(NCH - 1, MT - 1)]), (semLV, lw0)], inc=semV)
        c2 = sch.op("vector", lambda h: h.tensor_scalar(
            contrib[:, MT:2 * MT], pk0tot[:, :], selt[:, 0:1], None, ALU.mult),
            waits=[(semV, vsum), (semLV, lw0)], inc=semV)
        g1 = sch.op("gpsimd", lambda h: h.dma_start(out=cc_in.ap(), in_=contrib[:, :]),
                    waits=[(semV, c1), (semV, c2)], inc=semCCD, inc_n=16)
        gcc = sch.op("gpsimd", lambda h: h.collective_compute(
            "AllReduce", ALU.add, replica_groups=cfg["pairs"],
            ins=[cc_in.ap().opt()], outs=[cc_out.ap().opt()]),
            waits=[(semCCD, g1)], inc=semCC)
        g2 = sch.op("gpsimd", lambda h: h.dma_start(out=carryraw[:, :], in_=cc_out.ap()),
                    waits=[(semCC, gcc)], inc=semCCD, inc_n=16)
        vcD = sch.op("vector", lambda h: h.tensor_scalar(
            cD[:, :], carryraw[:, 0:MT], selt[:, 1:2], None, ALU.mult),
            waits=[(semCCD, g2)], inc=semV)
        vcP = sch.op("vector", lambda h: h.tensor_scalar(
            cP[:, :], carryraw[:, MT:2 * MT], selt[:, 1:2], None, ALU.mult),
            waits=[(semCCD, g2)], inc=semV)

        A_done_V = sch.val(semV)
        A_done_S = sch.val(semS)
        A_done_T = sch.val(semT)
        A_done_SPD = sch.val(semSPD)
        A_done_SPK = sch.val(semSPK)
        A_GATE = [(semV, A_done_V), (semS, A_done_S), (semT, A_done_T), (semSPD, A_done_SPD), (semSPK, A_done_SPK)]

        wv_fold, cv_v, vv_s = prep_matrix(
            wvT, W0, wstageB, rawB_B, True, True, cvT, bvT, True, vvB,
            stage_guard=[(semS, vk_s), (semV, wka_fold[-1])], fold_gate=[(semT, KEYS_A_DONE)])
        WV_READY = [(semV, wv_fold[k]) for k in range(KT)]
        VV_READY = (semS, vv_s)
        CV_READY = (semV, cv_v)

        XSTAGE_FREE = (semS, XBc[(NCH - 1, KT - 1)])
        kron_gate = [(semKL, KRON_LOADED), XSTAGE_FREE]
        if not KRON_IN_XSTAGE:
            kron_gate += [(semS, vv_s), (semV, cv_v), (semT, A_done_T)]
        group_cast = {}
        gidx = 0
        for t in range(T2):
            for half in range(2):
                p = gidx % 2
                acc = kronACC[p]
                tmp = kronTMP[p]
                prev_mul = None
                for c in range(KC):
                    off = ((half * KC + c) * T2 + t) * BB
                    boff = (half * KC + c) * 16
                    a_bc = atg_s[:, off:off + BB].unsqueeze(2).broadcast_to([128, BB, 16])
                    b_bc = btg_s[:, boff:boff + 16].unsqueeze(1).broadcast_to([128, BB, 16])
                    if c == 0:
                        waits = list(kron_gate)
                        if gidx - 2 in group_cast:
                            waits.append((semS, group_cast[gidx - 2]))
                        prev_mul = sch.op("vector", lambda h, a_=a_bc, b_=b_bc, o_=acc: h.tensor_tensor(
                            o_[:, :].rearrange("p (i e) -> p i e", e=16), a_, b_, ALU.mult),
                            waits=waits, inc=semV)
                    else:
                        vm = sch.op("vector", lambda h, a_=a_bc, b_=b_bc, o_=tmp: h.tensor_tensor(
                            o_[:, :].rearrange("p (i e) -> p i e", e=16), a_, b_, ALU.mult),
                            waits=list(kron_gate) + [(semV, prev_mul)], inc=semV)
                        prev_mul = sch.op("vector", lambda h, o_=acc, i_=tmp: h.tensor_add(
                            o_[:, :], o_[:, :], i_[:, :]),
                            waits=[(semV, vm)], inc=semV)
                if half == 0:
                    casts = [(t, 0, 1.0), (T2 + t, D2, 1.0)]
                else:
                    casts = [(t, D2, 1.0), (T2 + t, 0, -1.0)]
                for d_, o_, s_ in casts:
                    prev_acc = sch.op("scalar", lambda h, dd=d_, oo=o_, ss=s_, a_=acc: h.mul(
                        out=W1(dd)[:, oo:oo + D2], in_=a_[:, :], mul=ss),
                        waits=[(semV, prev_mul)] +
                              ([(semT, A_done_T)] if (t == 0 and half == 0) else []),
                        inc=semS)
                group_cast[gidx] = prev_acc
                gidx += 1
        WFFN_ALL = (semS, prev_acc)
        KRON_V_DONE = (semV, sch.val(semV))
        vlast = None
        nhalf = (D + PSW - 1) // PSW
        for hh in range(nhalf):
            w = min(PSW, D - PSW * hh)
            for k in range(KT):
                waits = [WFFN_ALL, (semV, V_const)]
                if vlast is not None:
                    waits.append((semS, vlast))
                else:
                    waits.append((semS, vv_s))
                sch.op("tensor", lambda h, k_=k, hh_=hh, w_=w: h.matmul(
                    ps_misc[0:1, 0:w_], onesB[:, :], W1(k_)[:, PSW * hh_:PSW * hh_ + w_],
                    start=(k_ == 0), stop=(k_ == KT - 1)),
                    waits=waits, inc=semT if k == KT - 1 else None)
            tps = sch.val(semT)
            vlast = sch.op("scalar", lambda h, hh_=hh, w_=w: h.copy(
                out=vfB[0:1, PSW * hh_:PSW * hh_ + w_], in_=ps_misc[0:1, 0:w_]),
                waits=[(semT, tps)], inc=semS)
        VF_READY = (semS, vlast)

        RK = {}; RDn = {}; RX = {}; WBv = {}; SC2v = {}; PKSv = {}
        RK_done = {}; RD_done = {}; semRX = [semRXa, semRXb]
        AS1 = {}; AS2 = {}; AS3 = {}; VEv = {}; FEv = {}; VBs = {}; FBs = {}
        OUTg = {}; OUT_done = {}; sbc2_ev = {}; s1bc_ev = {}; FPSd = {}

        for j in range(NCH):
            for t in range(MT):
                waits = list(A_GATE)
                if j > 0:
                    waits.append((semV, SC2v[(j - 1, t)]))
                RK[(j, t)] = sch.op("sync", lambda h, j_=j, t_=t: h.dma_start(
                    out=kkB[t_][:, :], in_=sc_kk[j_, 128 * t_:128 * (t_ + 1), :]),
                    waits=waits, inc=semRK, inc_n=16)
                waits = list(A_GATE)
                if j > 0:
                    waits.append((semV, AS2[(j - 1, t)]))
                RDn[(j, t)] = sch.op("sync", lambda h, j_=j, t_=t: h.dma_start(
                    out=denR[t_][:, :], in_=sc_den[j_, 128 * t_:128 * (t_ + 1), :]),
                    waits=waits, inc=semRD, inc_n=16)
            RK_done[j] = sch.val(semRK)
            RD_done[j] = sch.val(semRD)
            t_bc = sch.op("tensor", lambda h, j_=j: h.matmul(
                ps_bc[:, :], onesF[0:1, :], s_row[0:1, C * j_:C * (j_ + 1)], start=True, stop=True),
                waits=[(semT, A_done_T), (semS, A_done_S)] +
                      ([(semS, s1bc_ev[j - 1])] if j > 0 else []),
                inc=semT)
            sbc2_ev[j] = sch.op("scalar", lambda h: h.copy(out=sbcB[:, :], in_=ps_bc[:, :]),
                                waits=[(semT, t_bc), (semS, A_done_S)] +
                                      ([(semV, VEv[(j - 1, MT - 1)])] if j > 0 else []),
                                inc=semS)
            t_bc1 = sch.op("tensor", lambda h, j_=j: h.matmul(
                ps_bc[:, :], onesF[0:1, :], s1_row[0:1, C * j_:C * (j_ + 1)], start=True, stop=True),
                waits=[(semS, sbc2_ev[j])], inc=semT)
            s1bc_ev[j] = sch.op("scalar", lambda h: h.copy(out=s1bcB[:, :], in_=ps_bc[:, :]),
                                waits=[(semT, t_bc1)] +
                                      ([(semV, FEv[(j - 1, MT - 1)])] if j > 0 else
                                       [KRON_V_DONE]),
                                inc=semS)
            for m in range(MT):
                bank = pool[m % 2]
                for k in range(KT):
                    waits = [WV_READY[k]]
                    if k == 0:
                        prev = VEv.get((j, m - 2), VEv.get((j - 1, MT - 2 + m)))
                        if prev:
                            waits.append((semV, prev))
                        else:
                            waits.append((semV, A_done_V))
                    sch.op("tensor", lambda h, j_=j, m_=m, k_=k, b=bank: h.matmul(
                        b[:, :], W0(k_)[:, 128 * m_:128 * (m_ + 1)], xB(k_)[:, C * j_:C * (j_ + 1)],
                        start=(k_ == 0), stop=False),
                        waits=waits)
                VPS = sch.op("tensor", lambda h, j_=j, m_=m, b=bank: h.matmul(
                    b[:, :], vvB[0:1, 128 * m_:128 * (m_ + 1)], smB_row[0:1, C * j_:C * (j_ + 1)],
                    start=False, stop=True),
                    waits=[VV_READY], inc=semT)
                waits = [(semT, VPS), (semS, sbc2_ev[j]), (semT, A_done_T),
                         (semS, A_done_S), (semV, A_done_V)]
                if j > 0:
                    waits.append((semV, AS1[(j - 1, m)]))
                else:
                    waits += [KRON_V_DONE, WFFN_ALL]
                VEv[(j, m)] = sch.op("vector", lambda h, m_=m, b=bank: h.tensor_mul(
                    vbuf[m_][:, :], b[:, :], sbcB[:, :]),
                    waits=waits, inc=semV)
                VBs[(j, m)] = sch.op("scalar", lambda h, m_=m: h.add(
                    out=vbuf[m_][:, :], in_=vbuf[m_][:, :], add=cvT[:, m_:m_ + 1]),
                    waits=[(semV, VEv[(j, m)]), CV_READY], inc=semS)
            def emit_ffn_group(j, m):
                bank = pool[2 + m % 2]
                for k in range(KT):
                    waits = [WFFN_ALL]
                    if k == 0:
                        prev = FEv.get((j, m - 2), FEv.get((j - 1, MT - 2 + m)))
                        if prev:
                            waits.append((semV, prev))
                    sch.op("tensor", lambda h, j_=j, m_=m, k_=k, b=bank: h.matmul(
                        b[:, :], W1(k_)[:, 128 * m_:128 * (m_ + 1)], xB(k_)[:, C * j_:C * (j_ + 1)],
                        start=(k_ == 0), stop=False),
                        waits=waits)
                FPSd[(j, m)] = sch.op("tensor", lambda h, j_=j, m_=m, b=bank: h.matmul(
                    b[:, :], vfB[0:1, 128 * m_:128 * (m_ + 1)], s1mB_row[0:1, C * j_:C * (j_ + 1)],
                    start=False, stop=True),
                    waits=[VF_READY], inc=semT)

            for m in range(min(2, MT)):
                emit_ffn_group(j, m)
            DFv = {}
            for t in range(MT):
                DFv[t] = sch.op("vector", lambda h, t_=t: h.tensor_scalar(
                    denR[t_][:, :], denR[t_][:, :], cD[:, t_:t_ + 1], None, ALU.add),
                    waits=[(semRD, RD_done[j]), (semV, vcD)], inc=semV)
            RFs = {}
            for t in range(MT):
                rf0 = sch.op("scalar", lambda h, t_=t: h.activation(
                    out=denR[t_][:, :], in_=denR[t_][:, :], func=AF.Ln),
                    waits=[(semV, DFv[t])], inc=semS)
                RFs[t] = sch.op("scalar", lambda h, t_=t: h.activation(
                    out=denR[t_][:, :], in_=denR[t_][:, :], func=AF.Exp, scale=-1.0),
                    waits=[(semS, rf0)], inc=semS)

            for t in range(MT):
                waits = list(A_GATE)
                gi = j * MT + t
                if gi >= 2:
                    pj, pt = divmod(gi - 2, MT)
                    waits.append((semG, AS3[(pj, pt)]))
                else:
                    waits += [KRON_V_DONE, WFFN_ALL]
                RX[(j, t)] = sch.op("scalar", lambda h, j_=j, t_=t: h.dma_start(
                    out=xres[t_ % 2][:, :], in_=xT[128 * t_:128 * (t_ + 1), C * j_:C * (j_ + 1)]),
                    waits=waits, inc=semRX[gi % 2], inc_n=16)
                WBv[(j, t)] = sch.op("vector", lambda h, t_=t: h.tensor_mul(
                    kkB[t_][:, :], kkB[t_][:, :], denR[t_][:, :]),
                    waits=[(semRK, RK_done[j]), (semS, RFs[t])], inc=semV)
                waits = [(semV, WBv[(j, t)]), (semV, V_const)]
                if j > 0:
                    waits += [(semV, PKSv[(j - 1, t)]), (semOUT, OUT_done[j - 1])]
                sc2 = sch.op("vector", lambda h, t_=t: h.tensor_tensor_scan(
                    pkbuf[t_][:, :], onesC[:, :], kkB[t_][:, :], pkS[:, t_:t_ + 1],
                    ALU.mult, ALU.add),
                    waits=waits, inc=semV)
                SC2v[(j, t)] = sc2
                PKSv[(j, t)] = sch.op("vector", lambda h, t_=t: h.tensor_copy(
                    pkS[:, t_:t_ + 1], pkbuf[t_][:, C - 1:C]),
                    waits=[(semV, sc2)], inc=semV)
                FEv[(j, t)] = sch.op("vector", lambda h, t_=t, b=pool[2 + t % 2]: h.tensor_mul(
                    denR[t_][:, :], b[:, :], s1bcB[:, :]),
                    waits=[(semT, FPSd[(j, t)]), (semS, s1bc_ev[j]), (semV, WBv[(j, t)])],
                    inc=semV)
                FBs[(j, t)] = sch.op("scalar", lambda h, t_=t: h.add(
                    out=denR[t_][:, :], in_=denR[t_][:, :], add=cfT[:, t_:t_ + 1]),
                    waits=[(semV, FEv[(j, t)])], inc=semS)
                if t + 2 < MT:
                    emit_ffn_group(j, t + 2)
                AS1[(j, t)] = sch.op("vector", lambda h, t_=t: h.scalar_tensor_tensor(
                    pkbuf[t_][:, :], pkbuf[t_][:, :], cP[:, t_:t_ + 1], vbuf[t_][:, :],
                    ALU.add, ALU.mult),
                    waits=[(semV, PKSv[(j, t)]), (semS, VBs[(j, t)]), (semV, vcP)],
                    inc=semV)
                AS2[(j, t)] = sch.op("vector", lambda h, t_=t: h.tensor_add(
                    pkbuf[t_][:, :], pkbuf[t_][:, :], denR[t_][:, :]),
                    waits=[(semV, AS1[(j, t)]), (semS, FBs[(j, t)])], inc=semV)
                AS3[(j, t)] = sch.op("gpsimd", lambda h, t_=t: h.tensor_add(
                    pkbuf[t_][:, :], pkbuf[t_][:, :], xres[t_ % 2][:, :]),
                    waits=[(semV, AS2[(j, t)]), (semRX[(j * MT + t) % 2], RX[(j, t)])], inc=semG)
                OUTg[(j, t)] = sch.op("gpsimd", lambda h, j_=j, t_=t: h.dma_start(
                    out=outT[128 * t_:128 * (t_ + 1), C * j_:C * (j_ + 1)], in_=pkbuf[t_][:, :]),
                    waits=[(semG, AS3[(j, t)])], inc=semOUT, inc_n=16)
            OUT_done[j] = sch.val(semOUT)

        final_out = sch.val(semOUT)
        sch.raw("gpsimd", lambda h: h.wait_ge(semOUT, final_out))

        with nc.Block() as block:
            @block.sync
            def _(eng):
                for f in sch.ops["sync"]:
                    f(eng)

            @block.scalar
            def _(eng):
                for f in sch.ops["scalar"]:
                    f(eng)

            @block.vector
            def _(eng):
                for f in sch.ops["vector"]:
                    f(eng)

            @block.tensor
            def _(eng):
                for f in sch.ops["tensor"]:
                    f(eng)

            @block.gpsimd
            def _(eng):
                for f in sch.ops["gpsimd"]:
                    f(eng)

    return nc



def _col_tile(v, KT):
    return np.ascontiguousarray(np.asarray(v, np.float32).reshape(KT, 128).T)


def prep_inputs(cfg, x, Wk, bk, Wka, bka, Wv, bv, gamma, beta,
                G1_R, G2_R, G1_I, G2_I, bias_real, bias_imag):
    D = cfg["D"]; C = cfg["C"]; NCH = cfg["NCH"]; n_cores = cfg["n_cores"]
    D2 = cfg["D2"]; KC = cfg["KC"]
    KT = D // 128
    TCORE = C * NCH
    gg, ee = 16, 16

    xf = np.asarray(x, np.float32).reshape(-1, D)
    assert xf.shape[0] == n_cores * TCORE

    def expand(G1, G2):
        KA = np.ascontiguousarray(np.asarray(G1, np.float32)[0].transpose(1, 2, 0))
        KB = np.ascontiguousarray(np.asarray(G2, np.float32)[:, :, 0, :].transpose(0, 2, 1))
        at = np.stack([np.repeat(KA[c], gg, axis=0) for c in range(KC)])
        bt = np.stack([np.tile(KB[c], (128 // gg, 1)) for c in range(KC)])
        return np.ascontiguousarray(at), np.ascontiguousarray(bt)

    atR, btR = expand(G1_R, G2_R)
    atI, btI = expand(G1_I, G2_I)
    cf = np.concatenate([np.asarray(bias_real, np.float32), np.asarray(bias_imag, np.float32)])

    shared = dict(
        wkT=np.ascontiguousarray(np.asarray(Wk, np.float32).T),
        wkaT=np.ascontiguousarray(np.asarray(Wka, np.float32).T),
        wvT=np.ascontiguousarray(np.asarray(Wv, np.float32).T),
        gammaC=_col_tile(gamma, KT), betaC=_col_tile(beta, KT),
        bkC=_col_tile(bk, KT), bkaC=_col_tile(bka, KT), bvC=_col_tile(bv, KT),
        cfC=_col_tile(cf, KT),
        atgR=atR, atgI=atI, btgR=btR, btgI=btI,
    )
    in_maps = []
    for c in range(n_cores):
        sel = np.zeros((128, 2), np.float32)
        sel[:, 0] = 1.0 if c % 2 == 0 else 0.0
        sel[:, 1] = 0.0 if c % 2 == 0 else 1.0
        m = dict(shared)
        m["selv"] = sel
        m["xT"] = np.ascontiguousarray(xf[c * TCORE:(c + 1) * TCORE].T)
        in_maps.append(m)
    return in_maps


def kernel(L0_attn_hidden_states, L0_attn_attention_mask, L0_attn_Wk, L0_attn_bk,
           L0_attn_Wka, L0_attn_bka, L0_attn_Wv, L0_attn_bv, L0_attn_gamma,
           L0_attn_beta, L0_ffn_G1_R, L0_ffn_G2_R, L0_ffn_G1_I, L0_ffn_G2_I,
           L0_ffn_bias_real, L0_ffn_bias_imag):
    cfg = FULL_CFG
    x = np.asarray(L0_attn_hidden_states, np.float32)
    B, S, D = x.shape
    in_maps = prep_inputs(cfg, x, L0_attn_Wk, L0_attn_bk, L0_attn_Wka, L0_attn_bka,
                          L0_attn_Wv, L0_attn_bv, L0_attn_gamma, L0_attn_beta,
                          L0_ffn_G1_R, L0_ffn_G2_R, L0_ffn_G1_I, L0_ffn_G2_I,
                          L0_ffn_bias_real, L0_ffn_bias_imag)
    nc = build_program(cfg)
    res = run_bass_kernel_spmd(nc, in_maps, core_ids=list(range(cfg["n_cores"])))
    parts = [np.asarray(res.results[c]["outT"]).T for c in range(cfg["n_cores"])]
    out = np.concatenate(parts, axis=0).reshape(B, S, D).astype(np.float32)
    return out


# version 07
# speedup vs baseline: 1.2120x; 1.2120x over previous
import numpy as np
import ml_dtypes
import concourse.bass as bass
import concourse.mybir as mybir
from concourse.bass_utils import run_bass_kernel_spmd

F32 = mybir.dt.float32
BF16 = mybir.dt.bfloat16
AF = mybir.ActivationFunctionType
ALU = mybir.AluOpType

EPS = 1e-5

FULL_CFG = dict(
    D=1024,
    C=512,
    NCH=4,
    n_cores=8,
    pairs=[[0, 1], [2, 3], [4, 5], [6, 7]],
    D2=512,
    KC=4,
)


class Arena:

    def __init__(self, tensor, words):
        self.t = tensor
        self.words = words

    def f32(self, off, n):
        assert off + n <= self.words, (off, n, self.words)
        return self.t[:, off:off + n]

    def bf16(self, off, nwords):
        assert off + nwords <= self.words
        return self.t[:, off:off + nwords].bitcast(BF16)


class Sched:

    ENGS = ("sync", "scalar", "vector", "tensor", "gpsimd")

    def __init__(self):
        self.ops = {e: [] for e in self.ENGS}
        self.cnt = {}

    def val(self, sem):
        return self.cnt.get(id(sem), 0)

    def op(self, eng, fn, waits=(), inc=None, inc_n=1):
        waits = [(s, v) for (s, v) in waits if v]
        newv = None
        if inc is not None:
            self.cnt[id(inc)] = self.cnt.get(id(inc), 0) + inc_n
            newv = self.cnt[id(inc)]

        def emit(h, _waits=waits, _fn=fn, _inc=inc, _inc_n=inc_n):
            for s, v in _waits:
                h.wait_ge(s, v)
            instr = _fn(h)
            if _inc is not None:
                instr.then_inc(_inc, _inc_n)

        self.ops[eng].append(emit)
        return newv

    def raw(self, eng, fn):
        self.ops[eng].append(fn)


def build_program(cfg):
    D = cfg["D"]; C = cfg["C"]; NCH = cfg["NCH"]
    D2 = cfg["D2"]; KC = cfg["KC"]
    KT = D // 128
    MT = D // 128
    T2 = D2 // 128
    TCORE = C * NCH
    EXP_SCALE = 1.0 / float(np.sqrt(D))
    RD = 1.0 / D
    PSW = min(512, max(D, 128))

    nc = bass.Bass("TRN2", target_bir_lowering=False)

    xT = nc.dram_tensor("xT", [D, TCORE], F32, kind="ExternalInput").ap()
    wkT = nc.dram_tensor("wkT", [D, D], F32, kind="ExternalInput").ap()
    wkaT = nc.dram_tensor("wkaT", [D, D], F32, kind="ExternalInput").ap()
    wvT = nc.dram_tensor("wvT", [D, D], F32, kind="ExternalInput").ap()
    selv = nc.dram_tensor("selv", [128, 2], F32, kind="ExternalInput").ap()
    gammaC = nc.dram_tensor("gammaC", [128, KT], F32, kind="ExternalInput").ap()
    betaC = nc.dram_tensor("betaC", [128, KT], F32, kind="ExternalInput").ap()
    bkC = nc.dram_tensor("bkC", [128, MT], F32, kind="ExternalInput").ap()
    bkaC = nc.dram_tensor("bkaC", [128, MT], F32, kind="ExternalInput").ap()
    bvC = nc.dram_tensor("bvC", [128, MT], F32, kind="ExternalInput").ap()
    cfC = nc.dram_tensor("cfC", [128, MT], F32, kind="ExternalInput").ap()
    BB = D2 // 16
    atgR = nc.dram_tensor("atgR", [KC, D2, BB], F32, kind="ExternalInput").ap()
    atgI = nc.dram_tensor("atgI", [KC, D2, BB], F32, kind="ExternalInput").ap()
    btgR = nc.dram_tensor("btgR", [KC, 128, 16], F32, kind="ExternalInput").ap()
    btgI = nc.dram_tensor("btgI", [KC, 128, 16], F32, kind="ExternalInput").ap()
    outT = nc.dram_tensor("outT", [D, TCORE], F32, kind="ExternalOutput").ap()
    sc_kk = nc.dram_tensor("sc_kk", [NCH, D, C], F32).ap()
    sc_den = nc.dram_tensor("sc_den", [NCH, D, C], F32).ap()
    cc_in = nc.dram_tensor("cc_in", [128, 2 * MT], F32)
    cc_out = nc.dram_tensor("cc_out", [128, 2 * MT], F32)

    sch = Sched()
    CW = C

    A_KEYS = 0
    A_KWKK = 8 * CW
    A_DEN = 16 * CW
    A_KEYSB = 24 * CW
    A_W0 = 24 * CW + 4 * CW
    A_XST = A_W0 + 2 * CW
    A_XSQ = 32 * CW
    A_SBC = 34 * CW
    B_XRES = 32 * CW
    B_SBC = 34 * CW
    B_S1BC = 35 * CW
    B_WST = 36 * CW
    AR_WORDS = B_WST + max(2 * D + D // 2, 5 * D2)

    import contextlib
    with contextlib.ExitStack() as _st:
        xB_t = _st.enter_context(nc.sbuf_tensor("xB", [128, KT * TCORE], BF16))
        Wbuf0_t = _st.enter_context(nc.sbuf_tensor("Wbuf0", [128, KT * D], BF16))
        Wbuf1_t = _st.enter_context(nc.sbuf_tensor("Wbuf1", [128, KT * D], BF16))
        selt = _st.enter_context(nc.sbuf_tensor("selt", [128, 2], F32))
        gammaT = _st.enter_context(nc.sbuf_tensor("gammaT", [128, KT], F32))
        betaT = _st.enter_context(nc.sbuf_tensor("betaT", [128, KT], F32))
        betaB = _st.enter_context(nc.sbuf_tensor("betaB", [128, KT], BF16))
        bkT = _st.enter_context(nc.sbuf_tensor("bkT", [128, MT], F32))
        bkaT = _st.enter_context(nc.sbuf_tensor("bkaT", [128, MT], F32))
        bvT = _st.enter_context(nc.sbuf_tensor("bvT", [128, MT], F32))
        cfT = _st.enter_context(nc.sbuf_tensor("cfT", [128, MT], F32))
        bka32T = _st.enter_context(nc.sbuf_tensor("bka32T", [128, MT], F32))
        ckT = _st.enter_context(nc.sbuf_tensor("ckT", [128, MT], F32))
        cvT = _st.enter_context(nc.sbuf_tensor("cvT", [128, MT], F32))
        vkB = _st.enter_context(nc.sbuf_tensor("vkB", [1, D], BF16))
        vvB = _st.enter_context(nc.sbuf_tensor("vvB", [1, D], BF16))
        vfB = _st.enter_context(nc.sbuf_tensor("vfB", [1, D], BF16))
        s_row = _st.enter_context(nc.sbuf_tensor("s_row", [1, TCORE], F32))
        s1_row = _st.enter_context(nc.sbuf_tensor("s1_row", [1, TCORE], F32))
        smB_row = _st.enter_context(nc.sbuf_tensor("smB_row", [1, TCORE], BF16))
        s1mB_row = _st.enter_context(nc.sbuf_tensor("s1mB_row", [1, TCORE], BF16))
        onesB = _st.enter_context(nc.sbuf_tensor("onesB", [128, 1], BF16))
        onesF = _st.enter_context(nc.sbuf_tensor("onesF", [1, 128], F32))
        onesC = _st.enter_context(nc.sbuf_tensor("onesC", [128, C], F32))
        epsT = _st.enter_context(nc.sbuf_tensor("epsT", [128, 1], F32))
        eps2T = _st.enter_context(nc.sbuf_tensor("eps2T", [128, 1], F32))
        atg_s = _st.enter_context(nc.sbuf_tensor("atg_s", [128, 2 * KC * T2 * BB], F32))
        btg_s = _st.enter_context(nc.sbuf_tensor("btg_s", [128, 2 * KC * 16], F32))
        denS = _st.enter_context(nc.sbuf_tensor("denS", [128, MT], F32))
        pkS = _st.enter_context(nc.sbuf_tensor("pkS", [128, MT], F32))
        pk0p = _st.enter_context(nc.sbuf_tensor("pk0p", [128, NCH * MT], F32))
        pk0tot = _st.enter_context(nc.sbuf_tensor("pk0tot", [128, MT], F32))
        contrib = _st.enter_context(nc.sbuf_tensor("contrib", [128, 2 * MT], F32))
        carryraw = _st.enter_context(nc.sbuf_tensor("carryraw", [128, 2 * MT], F32))
        cD = _st.enter_context(nc.sbuf_tensor("cD", [128, MT], F32))
        cP = _st.enter_context(nc.sbuf_tensor("cP", [128, MT], F32))
        stw = _st.enter_context(nc.sbuf_tensor("stw", [1, 5 * C], F32))
        arena_t = _st.enter_context(nc.sbuf_tensor("arena", [128, AR_WORDS], F32))
        pool0 = _st.enter_context(nc.psum_tensor("pool0", [128, C], F32))
        pool1 = _st.enter_context(nc.psum_tensor("pool1", [128, C], F32))
        pool2 = _st.enter_context(nc.psum_tensor("pool2", [128, C], F32))
        pool3 = _st.enter_context(nc.psum_tensor("pool3", [128, C], F32))
        ps_sx = _st.enter_context(nc.psum_tensor("ps_sx", [1, C], F32))
        ps_sxx = _st.enter_context(nc.psum_tensor("ps_sxx", [1, C], F32))
        ps_bc = _st.enter_context(nc.psum_tensor("ps_bc", [128, C], F32))
        ps_misc = _st.enter_context(nc.psum_tensor("ps_misc", [128, PSW], F32))
        semLXa = _st.enter_context(nc.semaphore("semLXa"))
        semLXb = _st.enter_context(nc.semaphore("semLXb"))
        semLXc = _st.enter_context(nc.semaphore("semLXc"))
        semLXd = _st.enter_context(nc.semaphore("semLXd"))
        semLV = _st.enter_context(nc.semaphore("semLV"))
        semKL = _st.enter_context(nc.semaphore("semKL"))
        semLWa = _st.enter_context(nc.semaphore("semLWa"))
        semLWb = _st.enter_context(nc.semaphore("semLWb"))
        semLW = _st.enter_context(nc.semaphore("semLW"))
        semRK = _st.enter_context(nc.semaphore("semRK"))
        semRD = _st.enter_context(nc.semaphore("semRD"))
        semRXa = _st.enter_context(nc.semaphore("semRXa"))
        semRXb = _st.enter_context(nc.semaphore("semRXb"))
        semG = _st.enter_context(nc.semaphore("semG"))
        semSPDa = _st.enter_context(nc.semaphore("semSPDa"))
        semSPDb = _st.enter_context(nc.semaphore("semSPDb"))
        semSPK = _st.enter_context(nc.semaphore("semSPK"))
        semOUT = _st.enter_context(nc.semaphore("semOUT"))
        semCC = _st.enter_context(nc.semaphore("semCC"))
        semCCD = _st.enter_context(nc.semaphore("semCCD"))
        semT = _st.enter_context(nc.semaphore("semT"))
        semV = _st.enter_context(nc.semaphore("semV"))
        semS = _st.enter_context(nc.semaphore("semS"))
        ar = Arena(arena_t, AR_WORDS)
        pool = [pool0, pool1, pool2, pool3]

        def xB(k):
            return xB_t[:, k * TCORE:(k + 1) * TCORE]

        def W0(k):
            return Wbuf0_t[:, k * D:(k + 1) * D]

        def W1(k):
            return Wbuf1_t[:, k * D:(k + 1) * D]

        keysT = [ar.f32(A_KEYS + t * CW, CW) for t in range(MT)]
        kwkk = [ar.f32(A_KWKK + t * CW, CW) for t in range(MT)]
        denomL = [ar.f32(A_DEN + t * CW, CW) for t in range(MT)]
        keysB = [ar.bf16(A_KEYSB + t * CW // 2, CW // 2) for t in range(MT)]
        w0tmp = [ar.f32(A_W0 + i * CW, CW) for i in range(2)]
        xstage = [ar.f32(A_XST, CW), ar.f32(A_XST + CW, CW),
                  ar.f32(33 * CW, CW), ar.f32(35 * CW, CW)]
        xsqB = [ar.bf16(A_XSQ + i * CW // 2, CW // 2) for i in range(2)]
        sbcA = ar.f32(A_SBC, CW)
        wstageA = [ar.f32(A_KEYS, D), ar.f32(A_KEYS + D, D)]
        rawB_A = ar.bf16(A_KEYS + 2 * D, D // 2)
        kkB = [ar.f32(A_KEYS + t * CW, CW) for t in range(MT)]
        denR = [ar.f32(A_KWKK + t * CW, CW) for t in range(MT)]
        pkbuf = [ar.f32(A_DEN + t * CW, CW) for t in range(MT)]
        vbuf = [ar.f32(A_KEYSB + t * CW, CW) for t in range(MT)]
        xres = [ar.f32(B_XRES + i * CW, CW) for i in range(2)]
        sbcB = ar.f32(B_SBC, CW)
        s1bcB = ar.f32(B_S1BC, CW)
        wstageB = [ar.f32(B_WST, D), ar.f32(B_WST + D, D)]
        rawB_B = ar.bf16(B_WST + 2 * D, D // 2)
        KRON_IN_XSTAGE = CW >= D2
        if KRON_IN_XSTAGE:
            kronACC = [ar.f32(A_XST, D2), ar.f32(A_XST + CW, D2)]
            kronTMP = [ar.f32(33 * CW, D2), ar.f32(35 * CW, D2)]
        else:
            kronACC = [ar.f32(B_WST, D2), ar.f32(B_WST + D2, D2)]
            kronTMP = [ar.f32(B_WST + 2 * D2, D2), ar.f32(B_WST + 3 * D2, D2)]
            assert 4 * D2 <= max(2 * D + D // 2, 5 * D2)

        def ws(i):
            return stw[0:1, i * C:(i + 1) * C]

        sch.op("vector", lambda h: h.memset(onesC[:], 1.0), inc=semV)
        sch.op("vector", lambda h: h.memset(onesB[:], 1.0), inc=semV)
        sch.op("vector", lambda h: h.memset(onesF[:], 1.0), inc=semV)
        sch.op("vector", lambda h: h.memset(epsT[:], EPS), inc=semV)
        sch.op("vector", lambda h: h.memset(eps2T[:], EPS * EPS), inc=semV)
        sch.op("vector", lambda h: h.memset(denS[:], 0.0), inc=semV)
        sch.op("vector", lambda h: h.memset(pkS[:], 0.0), inc=semV)
        V_const = sch.val(semV)

        def dma(eng, dst, src, sem, waits=()):
            return sch.op(eng, lambda h, d=dst, s=src: h.dma_start(out=d, in_=s),
                          waits=waits, inc=sem, inc_n=16)

        lw0 = dma("sync", selt[:, :], selv[:, :], semLV)
        dma("sync", gammaT[:, :], gammaC[:, :], semLV)
        dma("sync", betaT[:, :], betaC[:, :], semLV)
        dma("sync", bkT[:, :], bkC[:, :], semLV)
        dma("sync", bkaT[:, :], bkaC[:, :], semLV)
        dma("sync", bvT[:, :], bvC[:, :], semLV)
        lw_vecs = dma("sync", cfT[:, :], cfC[:, :], semLV)
        lw0 = lw_vecs
        semLX = [semLXa, semLXb, semLXc, semLXd]
        LXc = {}

        def emit_xload(gi, waits=()):
            j, t = divmod(gi, KT)
            LXc[(j, t)] = sch.op("gpsimd", lambda h, j_=j, t_=t, g_=gi: h.dma_start(
                out=xstage[g_ % 4][:, :], in_=xT[128 * t_:128 * (t_ + 1), C * j_:C * (j_ + 1)]),
                waits=waits, inc=semLX[gi % 4], inc_n=16)

        for gi in range(min(4, NCH * KT)):
            emit_xload(gi)

        S_beta = sch.op("scalar", lambda h: h.copy(out=betaB[:], in_=betaT[:]),
                        waits=[(semLV, lw_vecs)], inc=semS)
        S_bka32 = sch.op("scalar", lambda h: h.mul(out=bka32T[:], in_=bkaT[:], mul=EXP_SCALE),
                         waits=[(semLV, lw_vecs)], inc=semS)

        semLW = [semLWa, semLWb]

        def prep_matrix(matT, Wdst, stage, rawB, fold_gamma, want_cb, cb_out, bias_tile,
                        want_v, vB_out, stage_guard=None, fold_gate=None):
            fold_done = []
            last_ckmm = None
            for k in range(KT):
                waits = list(stage_guard) if stage_guard else []
                if len(fold_done) >= 2:
                    waits.append((semV, fold_done[-2]))
                if last_ckmm is not None:
                    waits.append((semT, last_ckmm))
                ld = sch.op("sync", lambda h, k_=k, w=matT: h.dma_start(
                    out=stage[k_ % 2][:, :], in_=w[128 * k_:128 * (k_ + 1), :]),
                    waits=waits, inc=semLW[k % 2], inc_n=16)
                if want_cb:
                    for m in range(MT):
                        inc = semT if m == MT - 1 else None
                        last_ckmm = sch.op("tensor", lambda h, m_=m, k_=k: h.matmul(
                            ps_misc[:, k_ * MT + m_:k_ * MT + m_ + 1],
                            stage[k_ % 2][:, 128 * m_:128 * (m_ + 1)],
                            betaT[:, k_:k_ + 1], start=True, stop=True),
                            waits=[(semLW[k % 2], ld), (semLV, lw_vecs)], inc=inc) or last_ckmm
                fg = list(fold_gate) if fold_gate else []
                if fold_gamma:
                    fv = sch.op("vector", lambda h, k_=k: h.tensor_scalar(
                        Wdst(k_)[:, :], stage[k_ % 2][:, :], gammaT[:, k_:k_ + 1], None, ALU.mult),
                        waits=[(semLW[k % 2], ld), (semLV, lw_vecs)] + fg, inc=semV)
                else:
                    fv = sch.op("vector", lambda h, k_=k: h.tensor_copy(
                        Wdst(k_)[:, :], stage[k_ % 2][:, :]),
                        waits=[(semLW[k % 2], ld)] + fg, inc=semV)
                fold_done.append(fv)
            ck_psum = sch.val(semT)
            cb_v = None
            if want_cb:
                cb_v = sch.op("vector", lambda h: h.tensor_add(cb_out[:, :], ps_misc[:, 0:MT], bias_tile[:, :]),
                              waits=[(semT, ck_psum), (semLV, lw_vecs)], inc=semV)
                for k in range(1, KT):
                    cb_v = sch.op("vector", lambda h, k_=k: h.tensor_add(
                        cb_out[:, :], cb_out[:, :], ps_misc[:, k_ * MT:(k_ + 1) * MT]),
                        waits=[(semV, cb_v)], inc=semV)
            vlast = None
            if want_v:
                nhalf = (D + PSW - 1) // PSW
                for hh in range(nhalf):
                    w = min(PSW, D - PSW * hh)
                    for k in range(KT):
                        waits = [(semV, V_const), (semV, fold_done[k])]
                        if vlast is not None:
                            waits.append((semS, vlast))
                        elif want_cb:
                            waits.append((semV, cb_v))
                        sch.op("tensor", lambda h, k_=k, hh_=hh, w_=w: h.matmul(
                            ps_misc[0:1, 0:w_], onesB[:, :], Wdst(k_)[:, PSW * hh_:PSW * hh_ + w_],
                            start=(k_ == 0), stop=(k_ == KT - 1)),
                            waits=waits, inc=semT if k == KT - 1 else None)
                    tps = sch.val(semT)
                    vlast = sch.op("scalar", lambda h, hh_=hh, w_=w: h.copy(
                        out=vB_out[0:1, PSW * hh_:PSW * hh_ + w_], in_=ps_misc[0:1, 0:w_]),
                        waits=[(semT, tps)], inc=semS)
            return fold_done, cb_v, vlast

        wk_fold, ck_v, vk_s = prep_matrix(wkT, W0, wstageA, rawB_A, True, True, ckT, bkT, True, vkB)
        WK_END = [(semV, sch.val(semV)), (semS, sch.val(semS)), (semT, sch.val(semT))]
        wka_fold, _, _ = prep_matrix(wkaT, W1, wstageA, rawB_A, False, False, None, None, False, None,
                                     stage_guard=WK_END)
        for half, (ag, bg) in enumerate(((atgR, btgR), (atgI, btgI))):
            for c in range(KC):
                for t in range(T2):
                    off = ((half * KC + c) * T2 + t) * BB
                    dma("sync", atg_s[:, off:off + BB], ag[c, 128 * t:128 * (t + 1), :], semKL)
                boff = (half * KC + c) * 16
                dma("sync", btg_s[:, boff:boff + 16], bg[c, :, :], semKL)
        KRON_LOADED = sch.val(semKL)
        WK_READY = [(semV, wk_fold[k]) for k in range(KT)]
        WKA_READY = [(semV, wka_fold[k]) for k in range(KT)]
        VK_READY = (semS, vk_s)
        CK_READY = (semV, ck_v)
        A_SPACE_GATE = [(semV, sch.val(semV)), (semS, sch.val(semS)), (semT, sch.val(semT))]

        XBc = {}; ke_evac = {}; KEYv = {}; KKv = {}; W0v = {}
        SPK = {}; SPD = {}; DENSv = {}; kw_s = {}
        sbc_ev = {}; stats_done = {}; sq = {}
        SPA_done = {}
        stats_mfree = {}
        SROWV = {}; SROWS = {}
        kaps_last = {}

        def emit_casts(j):
            for t in range(KT):
                gi0 = j * KT + t
                XBc[(j, t)] = sch.op("scalar", lambda h, j_=j, t_=t, g_=gi0: h.copy(
                    out=xB(t_)[:, C * j_:C * (j_ + 1)], in_=xstage[g_ % 4][:, :]),
                    waits=[(semLX[gi0 % 4], LXc[(j, t)])], inc=semS)
                gi = gi0 + 4
                if gi < NCH * KT:
                    emit_xload(gi, waits=[(semS, XBc[(j, t)])])

        def emit_stats(j):
            for t in range(KT):
                gi = j * KT + t
                waits = [(semS, XBc[(j, t)])]
                if gi >= 2:
                    waits.append((semT, sq[("mm", gi - 2)]))
                sq[(j, t)] = sch.op("scalar", lambda h, j_=j, t_=t: h.activation(
                    out=xsqB[t_ % 2][:, :], in_=xB(t_)[:, C * j_:C * (j_ + 1)], func=AF.Square),
                    waits=waits, inc=semS)
                sch.op("tensor", lambda h, j_=j, t_=t: h.matmul(
                    ps_sx[0:1, :], onesB[:, :], xB(t_)[:, C * j_:C * (j_ + 1)],
                    start=(t_ == 0), stop=(t_ == KT - 1)),
                    waits=[(semS, XBc[(j, t)]), (semV, V_const)] +
                          ([(semV, stats_mfree[j - 1])] if j > 0 else []))
                sq[("mm", gi)] = sch.op("tensor", lambda h, j_=j, t_=t: h.matmul(
                    ps_sxx[0:1, :], onesB[:, :], xsqB[t_ % 2][:, :],
                    start=(t_ == 0), stop=(t_ == KT - 1)),
                    waits=[(semS, sq[(j, t)])], inc=semT)
            STATS_PS = sch.val(semT)
            v_m = sch.op("vector", lambda h: h.tensor_scalar_mul(ws(0), ps_sx[0:1, :], RD),
                         waits=[(semT, STATS_PS)] +
                               ([(semV, stats_done[j - 1])] if j > 0 else []),
                         inc=semV)
            v_msq = sch.op("vector", lambda h: h.tensor_scalar_mul(ws(1), ps_sxx[0:1, :], RD),
                           waits=[(semT, STATS_PS)] +
                                 ([(semV, stats_done[j - 1])] if j > 0 else []),
                           inc=semV)
            stats_mfree[j] = v_msq
            v_m2 = sch.op("vector", lambda h: h.tensor_mul(ws(2), ws(0), ws(0)),
                          waits=[(semV, v_m)], inc=semV)
            v_var = sch.op("vector", lambda h: h.tensor_sub(ws(3), ws(1), ws(2)),
                           waits=[(semV, v_msq), (semV, v_m2)], inc=semV)
            s_sq1 = sch.op("scalar", lambda h: h.activation(out=ws(4), in_=ws(3), func=AF.Sqrt, bias=epsT[0:1, 0:1]),
                           waits=[(semV, v_var)], inc=semS)
            s_sqS = sch.op("scalar", lambda h: h.activation(out=ws(1), in_=ws(3), func=AF.Sqrt,
                                                            bias=eps2T[0:1, 0:1], scale=1.0 + EPS),
                           waits=[(semV, v_var)], inc=semS)
            v_s1 = sch.op("vector", lambda h, j_=j: h.reciprocal(
                out=s1_row[0:1, C * j_:C * (j_ + 1)], in_=ws(4)),
                waits=[(semS, s_sq1)], inc=semV)
            v_s = sch.op("vector", lambda h, j_=j: h.reciprocal(
                out=s_row[0:1, C * j_:C * (j_ + 1)], in_=ws(1)),
                waits=[(semS, s_sqS)], inc=semV)
            v_sm = sch.op("vector", lambda h, j_=j: h.tensor_mul(
                ws(2), s_row[0:1, C * j_:C * (j_ + 1)], ws(0)),
                waits=[(semV, v_s)], inc=semV)
            v_smB = sch.op("vector", lambda h, j_=j: h.tensor_scalar(
                smB_row[0:1, C * j_:C * (j_ + 1)], ws(2), -1.0, None, ALU.mult),
                waits=[(semV, v_sm)], inc=semV)
            v_s1m = sch.op("vector", lambda h, j_=j: h.tensor_mul(
                ws(4), s1_row[0:1, C * j_:C * (j_ + 1)], ws(0)),
                waits=[(semV, v_s1), (semV, v_sm)], inc=semV)
            v_s1mB = sch.op("vector", lambda h, j_=j: h.tensor_scalar(
                s1mB_row[0:1, C * j_:C * (j_ + 1)], ws(4), -1.0, None, ALU.mult),
                waits=[(semV, v_s1m)], inc=semV)
            stats_done[j] = v_s1mB
            SROWV[j] = v_s
            SROWS[j] = v_smB

        emit_casts(0)
        if NCH > 1:
            emit_casts(1)
        emit_stats(0)

        RLs = {}
        semSPD = [semSPDa, semSPDb]

        def emit_scan_tile(j, t):
            waits = [(semS, kw_s[(j, t)]), (semV, V_const)]
            if j > 0:
                waits += [(semV, DENSv[(j - 1, t)]), (semV, W0v[(j - 1, t)]),
                          (semSPD[t % 2], SPD[(j - 1, t)]), (semSPK, SPA_done[j - 1][1])]
            sc1 = sch.op("vector", lambda h, t_=t: h.tensor_tensor_scan(
                denomL[t_][:, :], onesC[:, :], kwkk[t_][:, :], denS[:, t_:t_ + 1],
                ALU.mult, ALU.add),
                waits=waits, inc=semV)
            DENSv[(j, t)] = sch.op("vector", lambda h, t_=t: h.tensor_copy(
                denS[:, t_:t_ + 1], denomL[t_][:, C - 1:C]),
                waits=[(semV, sc1)], inc=semV)
            waits = [(semV, sc1)]
            prevrl = RLs.get((j, t - 2), RLs.get((j - 1, MT - 2 + t)))
            if prevrl:
                waits.append((semS, prevrl))
            SPD[(j, t)] = sch.op("gpsimd", lambda h, j_=j, t_=t: h.dma_start(
                out=sc_den[j_, 128 * t_:128 * (t_ + 1), :], in_=denomL[t_][:, :]),
                waits=waits, inc=semSPD[t % 2], inc_n=16)
            KKv[(j, t)] = sch.op("vector", lambda h, t_=t: h.scalar_tensor_tensor(
                kwkk[t_][:, :], keysT[t_][:, :], ckT[:, t_:t_ + 1], kwkk[t_][:, :],
                ALU.add, ALU.mult),
                waits=[(semV, sc1), (semS, KEYv[(j, t)]), CK_READY], inc=semV)
            rl0 = sch.op("scalar", lambda h, t_=t: h.activation(
                out=denomL[t_][:, :], in_=denomL[t_][:, :], func=AF.Ln),
                waits=[(semSPD[t % 2], SPD[(j, t)]), (semV, DENSv[(j, t)])], inc=semS)
            RLs[(j, t)] = sch.op("scalar", lambda h, t_=t: h.activation(
                out=denomL[t_][:, :], in_=denomL[t_][:, :], func=AF.Exp, scale=-1.0),
                waits=[(semS, rl0)], inc=semS)
            SPK[(j, t)] = sch.op("gpsimd", lambda h, j_=j, t_=t: h.dma_start(
                out=sc_kk[j_, 128 * t_:128 * (t_ + 1), :], in_=kwkk[t_][:, :]),
                waits=[(semV, KKv[(j, t)])], inc=semSPK, inc_n=16)

        def emit_w0(j, t):
            waits = [(semV, KKv[(j, t)]), (semS, RLs[(j, t)])]
            prev = W0v.get((j, t - 2), W0v.get((j - 1, MT - 2 + t)))
            if prev:
                waits.append((semV, prev))
            W0v[(j, t)] = sch.op("vector", lambda h, j_=j, t_=t: h.scalar_tensor_tensor(
                w0tmp[t_ % 2][:, :], kwkk[t_][:, :], 1.0, denomL[t_][:, :],
                ALU.mult, ALU.mult,
                accum_out=pk0p[:, (j_ * MT + t_):(j_ * MT + t_ + 1)]),
                waits=waits, inc=semV)

        for j in range(NCH):
            t_bc = sch.op("tensor", lambda h, j_=j: h.matmul(
                ps_bc[:, :], onesF[0:1, :], s_row[0:1, C * j_:C * (j_ + 1)], start=True, stop=True),
                waits=[(semV, SROWV[j]), (semV, V_const)] +
                      ([(semS, sbc_ev[j - 1])] if j > 0 else []),
                inc=semT)
            sbc_ev[j] = sch.op("scalar", lambda h: h.copy(out=sbcA[:, :], in_=ps_bc[:, :]),
                               waits=[(semT, t_bc)] +
                                     ([(semV, ke_evac[(j - 1, MT - 1)])] if j > 0 else
                                      [(semV, wka_fold[-1])]),
                               inc=semS)

            for t in range(MT):
                if j > 0:
                    emit_scan_tile(j - 1, t)
                    if t >= 2:
                        emit_w0(j - 1, t - 2)
                m = t
                bank = pool[m % 2]
                for k in range(KT):
                    waits = [WK_READY[k], (semS, XBc[(j, k)])]
                    if k == 0:
                        prev = ke_evac.get((j, m - 2), ke_evac.get((j - 1, MT - 2 + m)))
                        if prev:
                            waits.append((semV, prev))
                    sch.op("tensor", lambda h, j_=j, m_=m, k_=k, b=bank: h.matmul(
                        b[:, :], W0(k_)[:, 128 * m_:128 * (m_ + 1)], xB(k_)[:, C * j_:C * (j_ + 1)],
                        start=(k_ == 0), stop=False),
                        waits=waits)
                KPS = sch.op("tensor", lambda h, j_=j, m_=m, b=bank: h.matmul(
                    b[:, :], vkB[0:1, 128 * m_:128 * (m_ + 1)], smB_row[0:1, C * j_:C * (j_ + 1)],
                    start=False, stop=True),
                    waits=[VK_READY, (semV, SROWS[j])], inc=semT)
                waits = [(semT, KPS), (semS, sbc_ev[j])]
                if j == 0:
                    waits += A_SPACE_GATE
                else:
                    waits.append((semV, KKv[(j - 1, m)]))
                ke_evac[(j, m)] = sch.op("vector", lambda h, m_=m, b=bank: h.tensor_mul(
                    keysT[m_][:, :], b[:, :], sbcA[:, :]),
                    waits=waits, inc=semV)
                waits = [(semV, ke_evac[(j, m)]), CK_READY]
                if j > 0:
                    waits.append((semT, kaps_last[j - 1]))
                KEYv[(j, m)] = sch.op("scalar", lambda h, m_=m: h.activation(
                    out=keysB[m_][:, :], in_=keysT[m_][:, :], func=AF.Identity, bias=ckT[:, m_:m_ + 1]),
                    waits=waits, inc=semS)
            if j > 0:
                emit_w0(j - 1, MT - 2)
                emit_w0(j - 1, MT - 1)
            if j == NCH - 1:
                KEYS_A_DONE = sch.val(semT)
            kw_free = sch.val(semSPK)
            kaps = {}
            for f in range(MT):
                bank = pool[2 + f % 2]
                for k in range(KT):
                    waits = [WKA_READY[k], (semS, KEYv[(j, k)])]
                    if k == 0:
                        prev = kw_s.get((j, f - 2), kw_s.get((j - 1, MT - 2 + f)))
                        if prev:
                            waits.append((semS, prev))
                    sch.op("tensor", lambda h, f_=f, k_=k, b=bank: h.matmul(
                        b[:, :], W1(k_)[:, 128 * f_:128 * (f_ + 1)], keysB[k_][:, :],
                        start=(k_ == 0), stop=(k_ == KT - 1)),
                        waits=waits, inc=(semT if k == KT - 1 else None))
                kaps[f] = sch.val(semT)
                waits = [(semT, kaps[f]), (semS, S_bka32)]
                if j > 0:
                    waits += [(semSPK, kw_free), (semV, W0v[(j - 1, f)])]
                kw_s[(j, f)] = sch.op("scalar", lambda h, f_=f: h.activation(
                    out=kwkk[f_][:, :], in_=pool[2 + f_ % 2][:, :], func=AF.Exp,
                    bias=bka32T[:, f_:f_ + 1], scale=EXP_SCALE),
                    waits=waits, inc=semS)
            kaps_last[j] = kaps[MT - 1]
            if j + 1 < NCH:
                emit_stats(j + 1)
            if j + 2 < NCH:
                emit_casts(j + 2)
            if j > 0:
                SPA_done[j - 1] = ((sch.val(semSPDa), sch.val(semSPDb)), sch.val(semSPK))

        for t in range(MT):
            emit_scan_tile(NCH - 1, t)
            if t >= 2:
                emit_w0(NCH - 1, t - 2)
        emit_w0(NCH - 1, MT - 2)
        emit_w0(NCH - 1, MT - 1)
        SPA_done[NCH - 1] = ((sch.val(semSPDa), sch.val(semSPDb)), sch.val(semSPK))

        vsum = None
        for j in range(NCH):
            waits = [(semV, W0v[(NCH - 1, MT - 1)])]
            if vsum:
                waits.append((semV, vsum))
            if j == 0:
                vsum = sch.op("vector", lambda h: h.tensor_copy(pk0tot[:, :], pk0p[:, 0:MT]),
                              waits=waits, inc=semV)
            else:
                vsum = sch.op("vector", lambda h, j_=j: h.tensor_add(
                    pk0tot[:, :], pk0tot[:, :], pk0p[:, j_ * MT:(j_ + 1) * MT]),
                    waits=waits, inc=semV)
        c1 = sch.op("vector", lambda h: h.tensor_scalar(
            contrib[:, 0:MT], denS[:, :], selt[:, 0:1], None, ALU.mult),
            waits=[(semV, DENSv[(NCH - 1, MT - 1)]), (semLV, lw0)], inc=semV)
        c2 = sch.op("vector", lambda h: h.tensor_scalar(
            contrib[:, MT:2 * MT], pk0tot[:, :], selt[:, 0:1], None, ALU.mult),
            waits=[(semV, vsum), (semLV, lw0)], inc=semV)
        g1 = sch.op("gpsimd", lambda h: h.dma_start(out=cc_in.ap(), in_=contrib[:, :]),
                    waits=[(semV, c1), (semV, c2)], inc=semCCD, inc_n=16)
        gcc = sch.op("gpsimd", lambda h: h.collective_compute(
            "AllReduce", ALU.add, replica_groups=cfg["pairs"],
            ins=[cc_in.ap().opt()], outs=[cc_out.ap().opt()]),
            waits=[(semCCD, g1)], inc=semCC)
        g2 = sch.op("gpsimd", lambda h: h.dma_start(out=carryraw[:, :], in_=cc_out.ap()),
                    waits=[(semCC, gcc)], inc=semCCD, inc_n=16)
        vcD = sch.op("vector", lambda h: h.tensor_scalar(
            cD[:, :], carryraw[:, 0:MT], selt[:, 1:2], None, ALU.mult),
            waits=[(semCCD, g2)], inc=semV)
        vcP = sch.op("vector", lambda h: h.tensor_scalar(
            cP[:, :], carryraw[:, MT:2 * MT], selt[:, 1:2], None, ALU.mult),
            waits=[(semCCD, g2)], inc=semV)

        A_done_V = sch.val(semV)
        A_done_S = sch.val(semS)
        A_done_T = sch.val(semT)
        A_done_SPDa = sch.val(semSPDa)
        A_done_SPDb = sch.val(semSPDb)
        A_done_SPK = sch.val(semSPK)
        A_GATE = [(semV, A_done_V), (semS, A_done_S), (semT, A_done_T), (semSPDa, A_done_SPDa), (semSPDb, A_done_SPDb), (semSPK, A_done_SPK)]

        wv_fold, cv_v, vv_s = prep_matrix(
            wvT, W0, wstageB, rawB_B, True, True, cvT, bvT, True, vvB,
            stage_guard=[(semS, vk_s), (semV, wka_fold[-1])], fold_gate=[(semT, KEYS_A_DONE)])
        WV_READY = [(semV, wv_fold[k]) for k in range(KT)]
        VV_READY = (semS, vv_s)
        CV_READY = (semV, cv_v)

        XSTAGE_FREE = (semS, XBc[(NCH - 1, KT - 1)])
        kron_gate = [(semKL, KRON_LOADED), XSTAGE_FREE]
        if not KRON_IN_XSTAGE:
            kron_gate += [(semS, vv_s), (semV, cv_v), (semT, A_done_T)]
        group_cast = {}
        gidx = 0
        for t in range(T2):
            for half in range(2):
                p = gidx % 2
                acc = kronACC[p]
                tmp = kronTMP[p]
                prev_mul = None
                for c in range(KC):
                    off = ((half * KC + c) * T2 + t) * BB
                    boff = (half * KC + c) * 16
                    a_bc = atg_s[:, off:off + BB].unsqueeze(2).broadcast_to([128, BB, 16])
                    b_bc = btg_s[:, boff:boff + 16].unsqueeze(1).broadcast_to([128, BB, 16])
                    if c == 0:
                        waits = list(kron_gate)
                        if gidx - 2 in group_cast:
                            waits.append((semS, group_cast[gidx - 2]))
                        prev_mul = sch.op("vector", lambda h, a_=a_bc, b_=b_bc, o_=acc: h.tensor_tensor(
                            o_[:, :].rearrange("p (i e) -> p i e", e=16), a_, b_, ALU.mult),
                            waits=waits, inc=semV)
                    else:
                        vm = sch.op("vector", lambda h, a_=a_bc, b_=b_bc, o_=tmp: h.tensor_tensor(
                            o_[:, :].rearrange("p (i e) -> p i e", e=16), a_, b_, ALU.mult),
                            waits=list(kron_gate) + [(semV, prev_mul)], inc=semV)
                        prev_mul = sch.op("vector", lambda h, o_=acc, i_=tmp: h.tensor_add(
                            o_[:, :], o_[:, :], i_[:, :]),
                            waits=[(semV, vm)], inc=semV)
                if half == 0:
                    casts = [(t, 0, 1.0), (T2 + t, D2, 1.0)]
                else:
                    casts = [(t, D2, 1.0), (T2 + t, 0, -1.0)]
                for d_, o_, s_ in casts:
                    prev_acc = sch.op("scalar", lambda h, dd=d_, oo=o_, ss=s_, a_=acc: h.mul(
                        out=W1(dd)[:, oo:oo + D2], in_=a_[:, :], mul=ss),
                        waits=[(semV, prev_mul)] +
                              ([(semT, A_done_T)] if (t == 0 and half == 0) else []),
                        inc=semS)
                group_cast[gidx] = prev_acc
                gidx += 1
        WFFN_ALL = (semS, prev_acc)
        KRON_V_DONE = (semV, sch.val(semV))
        vlast = None
        nhalf = (D + PSW - 1) // PSW
        for hh in range(nhalf):
            w = min(PSW, D - PSW * hh)
            for k in range(KT):
                waits = [WFFN_ALL, (semV, V_const)]
                if vlast is not None:
                    waits.append((semS, vlast))
                else:
                    waits.append((semS, vv_s))
                sch.op("tensor", lambda h, k_=k, hh_=hh, w_=w: h.matmul(
                    ps_misc[0:1, 0:w_], onesB[:, :], W1(k_)[:, PSW * hh_:PSW * hh_ + w_],
                    start=(k_ == 0), stop=(k_ == KT - 1)),
                    waits=waits, inc=semT if k == KT - 1 else None)
            tps = sch.val(semT)
            vlast = sch.op("scalar", lambda h, hh_=hh, w_=w: h.copy(
                out=vfB[0:1, PSW * hh_:PSW * hh_ + w_], in_=ps_misc[0:1, 0:w_]),
                waits=[(semT, tps)], inc=semS)
        VF_READY = (semS, vlast)

        RK = {}; RDn = {}; RX = {}; WBv = {}; SC2v = {}; PKSv = {}
        RK_done = {}; RD_done = {}; semRX = [semRXa, semRXb]
        AS1 = {}; AS2 = {}; AS3 = {}; VEv = {}; FEv = {}; VBs = {}; FBs = {}
        OUTg = {}; OUT_done = {}; sbc2_ev = {}; s1bc_ev = {}; FPSd = {}

        def emit_tail(j, t):
            AS2[(j, t)] = sch.op("vector", lambda h, t_=t: h.tensor_add(
                pkbuf[t_][:, :], pkbuf[t_][:, :], denR[t_][:, :]),
                waits=[(semV, AS1[(j, t)]), (semS, FBs[(j, t)])], inc=semV)
            AS3[(j, t)] = sch.op("gpsimd", lambda h, t_=t: h.tensor_add(
                pkbuf[t_][:, :], pkbuf[t_][:, :], xres[t_ % 2][:, :]),
                waits=[(semV, AS2[(j, t)]), (semRX[(j * MT + t) % 2], RX[(j, t)])], inc=semG)
            OUTg[(j, t)] = sch.op("gpsimd", lambda h, j_=j, t_=t: h.dma_start(
                out=outT[128 * t_:128 * (t_ + 1), C * j_:C * (j_ + 1)], in_=pkbuf[t_][:, :]),
                waits=[(semG, AS3[(j, t)])], inc=semOUT, inc_n=16)

        for j in range(NCH):
            for t in range(MT):
                waits = list(A_GATE)
                if j > 0:
                    waits.append((semV, SC2v[(j - 1, t)]))
                RK[(j, t)] = sch.op("sync", lambda h, j_=j, t_=t: h.dma_start(
                    out=kkB[t_][:, :], in_=sc_kk[j_, 128 * t_:128 * (t_ + 1), :]),
                    waits=waits, inc=semRK, inc_n=16)
                waits = list(A_GATE)
                if j > 0:
                    waits.append((semV, AS2[(j - 1, t)]))
                RDn[(j, t)] = sch.op("sync", lambda h, j_=j, t_=t: h.dma_start(
                    out=denR[t_][:, :], in_=sc_den[j_, 128 * t_:128 * (t_ + 1), :]),
                    waits=waits, inc=semRD, inc_n=16)
            RK_done[j] = sch.val(semRK)
            RD_done[j] = sch.val(semRD)
            t_bc = sch.op("tensor", lambda h, j_=j: h.matmul(
                ps_bc[:, :], onesF[0:1, :], s_row[0:1, C * j_:C * (j_ + 1)], start=True, stop=True),
                waits=[(semT, A_done_T), (semS, A_done_S)] +
                      ([(semS, s1bc_ev[j - 1])] if j > 0 else []),
                inc=semT)
            sbc2_ev[j] = sch.op("scalar", lambda h: h.copy(out=sbcB[:, :], in_=ps_bc[:, :]),
                                waits=[(semT, t_bc), (semS, A_done_S)] +
                                      ([(semV, VEv[(j - 1, MT - 1)])] if j > 0 else []),
                                inc=semS)
            t_bc1 = sch.op("tensor", lambda h, j_=j: h.matmul(
                ps_bc[:, :], onesF[0:1, :], s1_row[0:1, C * j_:C * (j_ + 1)], start=True, stop=True),
                waits=[(semS, sbc2_ev[j])], inc=semT)
            s1bc_ev[j] = sch.op("scalar", lambda h: h.copy(out=s1bcB[:, :], in_=ps_bc[:, :]),
                                waits=[(semT, t_bc1)] +
                                      ([(semV, FEv[(j - 1, MT - 1)])] if j > 0 else
                                       [KRON_V_DONE]),
                                inc=semS)
            for m in range(MT):
                bank = pool[m % 2]
                for k in range(KT):
                    waits = [WV_READY[k]]
                    if k == 0:
                        prev = VEv.get((j, m - 2), VEv.get((j - 1, MT - 2 + m)))
                        if prev:
                            waits.append((semV, prev))
                        else:
                            waits.append((semV, A_done_V))
                    sch.op("tensor", lambda h, j_=j, m_=m, k_=k, b=bank: h.matmul(
                        b[:, :], W0(k_)[:, 128 * m_:128 * (m_ + 1)], xB(k_)[:, C * j_:C * (j_ + 1)],
                        start=(k_ == 0), stop=False),
                        waits=waits)
                VPS = sch.op("tensor", lambda h, j_=j, m_=m, b=bank: h.matmul(
                    b[:, :], vvB[0:1, 128 * m_:128 * (m_ + 1)], smB_row[0:1, C * j_:C * (j_ + 1)],
                    start=False, stop=True),
                    waits=[VV_READY], inc=semT)
                waits = [(semT, VPS), (semS, sbc2_ev[j]), (semT, A_done_T),
                         (semS, A_done_S), (semV, A_done_V)]
                if j > 0:
                    waits.append((semV, AS1[(j - 1, m)]))
                else:
                    waits += [KRON_V_DONE, WFFN_ALL]
                VEv[(j, m)] = sch.op("vector", lambda h, m_=m, b=bank: h.tensor_mul(
                    vbuf[m_][:, :], b[:, :], sbcB[:, :]),
                    waits=waits, inc=semV)
                VBs[(j, m)] = sch.op("scalar", lambda h, m_=m: h.add(
                    out=vbuf[m_][:, :], in_=vbuf[m_][:, :], add=cvT[:, m_:m_ + 1]),
                    waits=[(semV, VEv[(j, m)]), CV_READY], inc=semS)
            def emit_ffn_group(j, m):
                bank = pool[2 + m % 2]
                for k in range(KT):
                    waits = [WFFN_ALL]
                    if k == 0:
                        prev = FEv.get((j, m - 2), FEv.get((j - 1, MT - 2 + m)))
                        if prev:
                            waits.append((semV, prev))
                    sch.op("tensor", lambda h, j_=j, m_=m, k_=k, b=bank: h.matmul(
                        b[:, :], W1(k_)[:, 128 * m_:128 * (m_ + 1)], xB(k_)[:, C * j_:C * (j_ + 1)],
                        start=(k_ == 0), stop=False),
                        waits=waits)
                FPSd[(j, m)] = sch.op("tensor", lambda h, j_=j, m_=m, b=bank: h.matmul(
                    b[:, :], vfB[0:1, 128 * m_:128 * (m_ + 1)], s1mB_row[0:1, C * j_:C * (j_ + 1)],
                    start=False, stop=True),
                    waits=[VF_READY], inc=semT)

            for m in range(min(2, MT)):
                emit_ffn_group(j, m)
            DFv = {}
            for t in range(MT):
                DFv[t] = sch.op("vector", lambda h, t_=t: h.tensor_scalar(
                    denR[t_][:, :], denR[t_][:, :], cD[:, t_:t_ + 1], None, ALU.add),
                    waits=[(semRD, RD_done[j]), (semV, vcD)], inc=semV)
            RFs = {}
            for t in range(MT):
                rf0 = sch.op("scalar", lambda h, t_=t: h.activation(
                    out=denR[t_][:, :], in_=denR[t_][:, :], func=AF.Ln),
                    waits=[(semV, DFv[t])], inc=semS)
                RFs[t] = sch.op("scalar", lambda h, t_=t: h.activation(
                    out=denR[t_][:, :], in_=denR[t_][:, :], func=AF.Exp, scale=-1.0),
                    waits=[(semS, rf0)], inc=semS)

            for t in range(MT):
                waits = list(A_GATE)
                gi = j * MT + t
                if gi >= 2:
                    pj, pt = divmod(gi - 2, MT)
                    waits.append((semG, AS3[(pj, pt)]))
                else:
                    waits += [KRON_V_DONE, WFFN_ALL]
                RX[(j, t)] = sch.op("scalar", lambda h, j_=j, t_=t: h.dma_start(
                    out=xres[t_ % 2][:, :], in_=xT[128 * t_:128 * (t_ + 1), C * j_:C * (j_ + 1)]),
                    waits=waits, inc=semRX[gi % 2], inc_n=16)
                WBv[(j, t)] = sch.op("vector", lambda h, t_=t: h.tensor_mul(
                    kkB[t_][:, :], kkB[t_][:, :], denR[t_][:, :]),
                    waits=[(semRK, RK_done[j]), (semS, RFs[t])], inc=semV)
                waits = [(semV, WBv[(j, t)]), (semV, V_const)]
                if j > 0:
                    waits += [(semV, PKSv[(j - 1, t)]), (semOUT, OUT_done[j - 1])]
                sc2 = sch.op("vector", lambda h, t_=t: h.tensor_tensor_scan(
                    pkbuf[t_][:, :], onesC[:, :], kkB[t_][:, :], pkS[:, t_:t_ + 1],
                    ALU.mult, ALU.add),
                    waits=waits, inc=semV)
                SC2v[(j, t)] = sc2
                PKSv[(j, t)] = sch.op("vector", lambda h, t_=t: h.tensor_copy(
                    pkS[:, t_:t_ + 1], pkbuf[t_][:, C - 1:C]),
                    waits=[(semV, sc2)], inc=semV)
                FEv[(j, t)] = sch.op("vector", lambda h, t_=t, b=pool[2 + t % 2]: h.tensor_mul(
                    denR[t_][:, :], b[:, :], s1bcB[:, :]),
                    waits=[(semT, FPSd[(j, t)]), (semS, s1bc_ev[j]), (semV, WBv[(j, t)])],
                    inc=semV)
                FBs[(j, t)] = sch.op("scalar", lambda h, t_=t: h.add(
                    out=denR[t_][:, :], in_=denR[t_][:, :], add=cfT[:, t_:t_ + 1]),
                    waits=[(semV, FEv[(j, t)])], inc=semS)
                if t + 2 < MT:
                    emit_ffn_group(j, t + 2)
                AS1[(j, t)] = sch.op("vector", lambda h, t_=t: h.scalar_tensor_tensor(
                    pkbuf[t_][:, :], pkbuf[t_][:, :], cP[:, t_:t_ + 1], vbuf[t_][:, :],
                    ALU.add, ALU.mult),
                    waits=[(semV, PKSv[(j, t)]), (semS, VBs[(j, t)]), (semV, vcP)],
                    inc=semV)
                if t >= 1:
                    emit_tail(j, t - 1)
            emit_tail(j, MT - 1)
            OUT_done[j] = sch.val(semOUT)

        final_out = sch.val(semOUT)
        sch.raw("gpsimd", lambda h: h.wait_ge(semOUT, final_out))

        with nc.Block() as block:
            @block.sync
            def _(eng):
                for f in sch.ops["sync"]:
                    f(eng)

            @block.scalar
            def _(eng):
                for f in sch.ops["scalar"]:
                    f(eng)

            @block.vector
            def _(eng):
                for f in sch.ops["vector"]:
                    f(eng)

            @block.tensor
            def _(eng):
                for f in sch.ops["tensor"]:
                    f(eng)

            @block.gpsimd
            def _(eng):
                for f in sch.ops["gpsimd"]:
                    f(eng)

    return nc



def _col_tile(v, KT):
    return np.ascontiguousarray(np.asarray(v, np.float32).reshape(KT, 128).T)


def prep_inputs(cfg, x, Wk, bk, Wka, bka, Wv, bv, gamma, beta,
                G1_R, G2_R, G1_I, G2_I, bias_real, bias_imag):
    D = cfg["D"]; C = cfg["C"]; NCH = cfg["NCH"]; n_cores = cfg["n_cores"]
    D2 = cfg["D2"]; KC = cfg["KC"]
    KT = D // 128
    TCORE = C * NCH
    gg, ee = 16, 16

    xf = np.asarray(x, np.float32).reshape(-1, D)
    assert xf.shape[0] == n_cores * TCORE

    def expand(G1, G2):
        KA = np.ascontiguousarray(np.asarray(G1, np.float32)[0].transpose(1, 2, 0))
        KB = np.ascontiguousarray(np.asarray(G2, np.float32)[:, :, 0, :].transpose(0, 2, 1))
        at = np.stack([np.repeat(KA[c], gg, axis=0) for c in range(KC)])
        bt = np.stack([np.tile(KB[c], (128 // gg, 1)) for c in range(KC)])
        return np.ascontiguousarray(at), np.ascontiguousarray(bt)

    atR, btR = expand(G1_R, G2_R)
    atI, btI = expand(G1_I, G2_I)
    cf = np.concatenate([np.asarray(bias_real, np.float32), np.asarray(bias_imag, np.float32)])

    shared = dict(
        wkT=np.ascontiguousarray(np.asarray(Wk, np.float32).T),
        wkaT=np.ascontiguousarray(np.asarray(Wka, np.float32).T),
        wvT=np.ascontiguousarray(np.asarray(Wv, np.float32).T),
        gammaC=_col_tile(gamma, KT), betaC=_col_tile(beta, KT),
        bkC=_col_tile(bk, KT), bkaC=_col_tile(bka, KT), bvC=_col_tile(bv, KT),
        cfC=_col_tile(cf, KT),
        atgR=atR, atgI=atI, btgR=btR, btgI=btI,
    )
    in_maps = []
    for c in range(n_cores):
        sel = np.zeros((128, 2), np.float32)
        sel[:, 0] = 1.0 if c % 2 == 0 else 0.0
        sel[:, 1] = 0.0 if c % 2 == 0 else 1.0
        m = dict(shared)
        m["selv"] = sel
        m["xT"] = np.ascontiguousarray(xf[c * TCORE:(c + 1) * TCORE].T)
        in_maps.append(m)
    return in_maps


def kernel(L0_attn_hidden_states, L0_attn_attention_mask, L0_attn_Wk, L0_attn_bk,
           L0_attn_Wka, L0_attn_bka, L0_attn_Wv, L0_attn_bv, L0_attn_gamma,
           L0_attn_beta, L0_ffn_G1_R, L0_ffn_G2_R, L0_ffn_G1_I, L0_ffn_G2_I,
           L0_ffn_bias_real, L0_ffn_bias_imag):
    cfg = FULL_CFG
    x = np.asarray(L0_attn_hidden_states, np.float32)
    B, S, D = x.shape
    in_maps = prep_inputs(cfg, x, L0_attn_Wk, L0_attn_bk, L0_attn_Wka, L0_attn_bka,
                          L0_attn_Wv, L0_attn_bv, L0_attn_gamma, L0_attn_beta,
                          L0_ffn_G1_R, L0_ffn_G2_R, L0_ffn_G1_I, L0_ffn_G2_I,
                          L0_ffn_bias_real, L0_ffn_bias_imag)
    nc = build_program(cfg)
    res = run_bass_kernel_spmd(nc, in_maps, core_ids=list(range(cfg["n_cores"])))
    parts = [np.asarray(res.results[c]["outT"]).T for c in range(cfg["n_cores"])]
    out = np.concatenate(parts, axis=0).reshape(B, S, D).astype(np.float32)
    return out


# version 08
# speedup vs baseline: 1.2578x; 1.0378x over previous
import numpy as np
import ml_dtypes
import concourse.bass as bass
import concourse.mybir as mybir
from concourse.bass_utils import run_bass_kernel_spmd

F32 = mybir.dt.float32
BF16 = mybir.dt.bfloat16
AF = mybir.ActivationFunctionType
ALU = mybir.AluOpType

EPS = 1e-5

FULL_CFG = dict(
    D=1024,
    C=512,
    NCH=4,
    n_cores=8,
    pairs=[[0, 1], [2, 3], [4, 5], [6, 7]],
    D2=512,
    KC=4,
)


class Arena:

    def __init__(self, tensor, words):
        self.t = tensor
        self.words = words

    def f32(self, off, n):
        assert off + n <= self.words, (off, n, self.words)
        return self.t[:, off:off + n]

    def bf16(self, off, nwords):
        assert off + nwords <= self.words
        return self.t[:, off:off + nwords].bitcast(BF16)


class Sched:

    ENGS = ("sync", "scalar", "vector", "tensor", "gpsimd")

    def __init__(self):
        self.ops = {e: [] for e in self.ENGS}
        self.cnt = {}

    def val(self, sem):
        return self.cnt.get(id(sem), 0)

    def op(self, eng, fn, waits=(), inc=None, inc_n=1):
        waits = [(s, v) for (s, v) in waits if v]
        newv = None
        if inc is not None:
            self.cnt[id(inc)] = self.cnt.get(id(inc), 0) + inc_n
            newv = self.cnt[id(inc)]

        def emit(h, _waits=waits, _fn=fn, _inc=inc, _inc_n=inc_n):
            for s, v in _waits:
                h.wait_ge(s, v)
            instr = _fn(h)
            if _inc is not None:
                instr.then_inc(_inc, _inc_n)

        self.ops[eng].append(emit)
        return newv

    def raw(self, eng, fn):
        self.ops[eng].append(fn)


def build_program(cfg):
    D = cfg["D"]; C = cfg["C"]; NCH = cfg["NCH"]
    D2 = cfg["D2"]; KC = cfg["KC"]
    KT = D // 128
    MT = D // 128
    T2 = D2 // 128
    TCORE = C * NCH
    EXP_SCALE = 1.0 / float(np.sqrt(D))
    RD = 1.0 / D
    PSW = min(512, max(D, 128))

    nc = bass.Bass("TRN2", target_bir_lowering=False)

    xT = nc.dram_tensor("xT", [D, TCORE], F32, kind="ExternalInput").ap()
    wkT = nc.dram_tensor("wkT", [D, D], F32, kind="ExternalInput").ap()
    wkaT = nc.dram_tensor("wkaT", [D, D], F32, kind="ExternalInput").ap()
    wvT = nc.dram_tensor("wvT", [D, D], F32, kind="ExternalInput").ap()
    selv = nc.dram_tensor("selv", [128, 2], F32, kind="ExternalInput").ap()
    gammaC = nc.dram_tensor("gammaC", [128, KT], F32, kind="ExternalInput").ap()
    betaC = nc.dram_tensor("betaC", [128, KT], F32, kind="ExternalInput").ap()
    bkC = nc.dram_tensor("bkC", [128, MT], F32, kind="ExternalInput").ap()
    bkaC = nc.dram_tensor("bkaC", [128, MT], F32, kind="ExternalInput").ap()
    bvC = nc.dram_tensor("bvC", [128, MT], F32, kind="ExternalInput").ap()
    cfC = nc.dram_tensor("cfC", [128, MT], F32, kind="ExternalInput").ap()
    BB = D2 // 16
    atgR = nc.dram_tensor("atgR", [KC, D2, BB], F32, kind="ExternalInput").ap()
    atgI = nc.dram_tensor("atgI", [KC, D2, BB], F32, kind="ExternalInput").ap()
    btgR = nc.dram_tensor("btgR", [KC, 128, 16], F32, kind="ExternalInput").ap()
    btgI = nc.dram_tensor("btgI", [KC, 128, 16], F32, kind="ExternalInput").ap()
    outT = nc.dram_tensor("outT", [D, TCORE], F32, kind="ExternalOutput").ap()
    sc_kk = nc.dram_tensor("sc_kk", [NCH, D, C], F32).ap()
    sc_den = nc.dram_tensor("sc_den", [NCH, D, C], F32).ap()
    cc_in = nc.dram_tensor("cc_in", [128, 2 * MT], F32)
    cc_out = nc.dram_tensor("cc_out", [128, 2 * MT], F32)

    sch = Sched()
    CW = C

    A_KEYS = 0
    A_KWKK = 8 * CW
    A_DEN = 16 * CW
    A_KEYSB = 24 * CW
    A_W0 = 24 * CW + 4 * CW
    A_XST = A_W0 + 2 * CW
    A_XSQ = 32 * CW
    A_SBC = 34 * CW
    B_XRES = 32 * CW
    B_SBC = 34 * CW
    B_S1BC = 35 * CW
    B_WST = 36 * CW
    AR_WORDS = B_WST + max(2 * D + D // 2, 5 * D2)

    import contextlib
    with contextlib.ExitStack() as _st:
        xB_t = _st.enter_context(nc.sbuf_tensor("xB", [128, KT * TCORE], BF16))
        Wbuf0_t = _st.enter_context(nc.sbuf_tensor("Wbuf0", [128, KT * D], BF16))
        Wbuf1_t = _st.enter_context(nc.sbuf_tensor("Wbuf1", [128, KT * D], BF16))
        selt = _st.enter_context(nc.sbuf_tensor("selt", [128, 2], F32))
        gammaT = _st.enter_context(nc.sbuf_tensor("gammaT", [128, KT], F32))
        betaT = _st.enter_context(nc.sbuf_tensor("betaT", [128, KT], F32))
        betaB = _st.enter_context(nc.sbuf_tensor("betaB", [128, KT], BF16))
        bkT = _st.enter_context(nc.sbuf_tensor("bkT", [128, MT], F32))
        bkaT = _st.enter_context(nc.sbuf_tensor("bkaT", [128, MT], F32))
        bvT = _st.enter_context(nc.sbuf_tensor("bvT", [128, MT], F32))
        cfT = _st.enter_context(nc.sbuf_tensor("cfT", [128, MT], F32))
        bka32T = _st.enter_context(nc.sbuf_tensor("bka32T", [128, MT], F32))
        ckT = _st.enter_context(nc.sbuf_tensor("ckT", [128, MT], F32))
        cvT = _st.enter_context(nc.sbuf_tensor("cvT", [128, MT], F32))
        vkB = _st.enter_context(nc.sbuf_tensor("vkB", [1, D], BF16))
        vvB = _st.enter_context(nc.sbuf_tensor("vvB", [1, D], BF16))
        vfB = _st.enter_context(nc.sbuf_tensor("vfB", [1, D], BF16))
        s_row = _st.enter_context(nc.sbuf_tensor("s_row", [1, TCORE], F32))
        s1_row = _st.enter_context(nc.sbuf_tensor("s1_row", [1, TCORE], F32))
        smB_row = _st.enter_context(nc.sbuf_tensor("smB_row", [1, TCORE], BF16))
        s1mB_row = _st.enter_context(nc.sbuf_tensor("s1mB_row", [1, TCORE], BF16))
        onesB = _st.enter_context(nc.sbuf_tensor("onesB", [128, 1], BF16))
        onesF = _st.enter_context(nc.sbuf_tensor("onesF", [1, 128], F32))
        onesC = _st.enter_context(nc.sbuf_tensor("onesC", [128, C], F32))
        epsT = _st.enter_context(nc.sbuf_tensor("epsT", [128, 1], F32))
        eps2T = _st.enter_context(nc.sbuf_tensor("eps2T", [128, 1], F32))
        atg_s = _st.enter_context(nc.sbuf_tensor("atg_s", [128, 2 * KC * T2 * BB], F32))
        btg_s = _st.enter_context(nc.sbuf_tensor("btg_s", [128, 2 * KC * 16], F32))
        denS = _st.enter_context(nc.sbuf_tensor("denS", [128, MT], F32))
        pkS = _st.enter_context(nc.sbuf_tensor("pkS", [128, MT], F32))
        pk0p = _st.enter_context(nc.sbuf_tensor("pk0p", [128, NCH * MT], F32))
        pk0tot = _st.enter_context(nc.sbuf_tensor("pk0tot", [128, MT], F32))
        contrib = _st.enter_context(nc.sbuf_tensor("contrib", [128, 2 * MT], F32))
        carryraw = _st.enter_context(nc.sbuf_tensor("carryraw", [128, 2 * MT], F32))
        cD = _st.enter_context(nc.sbuf_tensor("cD", [128, MT], F32))
        cP = _st.enter_context(nc.sbuf_tensor("cP", [128, MT], F32))
        stw = _st.enter_context(nc.sbuf_tensor("stw", [1, 5 * C], F32))
        arena_t = _st.enter_context(nc.sbuf_tensor("arena", [128, AR_WORDS], F32))
        pool0 = _st.enter_context(nc.psum_tensor("pool0", [128, C], F32))
        pool1 = _st.enter_context(nc.psum_tensor("pool1", [128, C], F32))
        pool2 = _st.enter_context(nc.psum_tensor("pool2", [128, C], F32))
        pool3 = _st.enter_context(nc.psum_tensor("pool3", [128, C], F32))
        ps_sx = _st.enter_context(nc.psum_tensor("ps_sx", [1, C], F32))
        ps_sxx = _st.enter_context(nc.psum_tensor("ps_sxx", [1, C], F32))
        ps_bc = _st.enter_context(nc.psum_tensor("ps_bc", [128, C], F32))
        ps_misc = _st.enter_context(nc.psum_tensor("ps_misc", [128, PSW], F32))
        semLXa = _st.enter_context(nc.semaphore("semLXa"))
        semLXb = _st.enter_context(nc.semaphore("semLXb"))
        semLXc = _st.enter_context(nc.semaphore("semLXc"))
        semLXd = _st.enter_context(nc.semaphore("semLXd"))
        semLV = _st.enter_context(nc.semaphore("semLV"))
        semKL = _st.enter_context(nc.semaphore("semKL"))
        semLWa = _st.enter_context(nc.semaphore("semLWa"))
        semLWb = _st.enter_context(nc.semaphore("semLWb"))
        semLW = _st.enter_context(nc.semaphore("semLW"))
        semRK = _st.enter_context(nc.semaphore("semRK"))
        semRD = _st.enter_context(nc.semaphore("semRD"))
        semRXa = _st.enter_context(nc.semaphore("semRXa"))
        semRXb = _st.enter_context(nc.semaphore("semRXb"))
        semG = _st.enter_context(nc.semaphore("semG"))
        semSPDa = _st.enter_context(nc.semaphore("semSPDa"))
        semSPDb = _st.enter_context(nc.semaphore("semSPDb"))
        semSPK = _st.enter_context(nc.semaphore("semSPK"))
        semOUT = _st.enter_context(nc.semaphore("semOUT"))
        semCC = _st.enter_context(nc.semaphore("semCC"))
        semCCD = _st.enter_context(nc.semaphore("semCCD"))
        semT = _st.enter_context(nc.semaphore("semT"))
        semV = _st.enter_context(nc.semaphore("semV"))
        semS = _st.enter_context(nc.semaphore("semS"))
        ar = Arena(arena_t, AR_WORDS)
        pool = [pool0, pool1, pool2, pool3]

        def xB(k):
            return xB_t[:, k * TCORE:(k + 1) * TCORE]

        def W0(k):
            return Wbuf0_t[:, k * D:(k + 1) * D]

        def W1(k):
            return Wbuf1_t[:, k * D:(k + 1) * D]

        keysT = [ar.f32(A_KEYS + t * CW, CW) for t in range(MT)]
        kwkk = [ar.f32(A_KWKK + t * CW, CW) for t in range(MT)]
        denomL = [ar.f32(A_DEN + t * CW, CW) for t in range(MT)]
        keysB = [ar.bf16(A_KEYSB + t * CW // 2, CW // 2) for t in range(MT)]
        w0tmp = [ar.f32(A_W0 + i * CW, CW) for i in range(2)]
        xstage = [ar.f32(A_XST, CW), ar.f32(A_XST + CW, CW),
                  ar.f32(33 * CW, CW), ar.f32(35 * CW, CW)]
        xsqB = [ar.bf16(A_XSQ + i * CW // 2, CW // 2) for i in range(2)]
        sbcA = ar.f32(A_SBC, CW)
        wstageA = [ar.f32(A_KEYS, D), ar.f32(A_KEYS + D, D)]
        rawB_A = ar.bf16(A_KEYS + 2 * D, D // 2)
        kkB = [ar.f32(A_KEYS + t * CW, CW) for t in range(MT)]
        denR = [ar.f32(A_KWKK + t * CW, CW) for t in range(MT)]
        pkbuf = [ar.f32(A_DEN + t * CW, CW) for t in range(MT)]
        vbuf = [ar.f32(A_KEYSB + t * CW, CW) for t in range(MT)]
        xres = [ar.f32(B_XRES + i * CW, CW) for i in range(2)]
        sbcB = ar.f32(B_SBC, CW)
        s1bcB = ar.f32(B_S1BC, CW)
        wstageB = [ar.f32(B_WST, D), ar.f32(B_WST + D, D)]
        rawB_B = ar.bf16(B_WST + 2 * D, D // 2)
        KRON_IN_XSTAGE = CW >= D2
        if KRON_IN_XSTAGE:
            kronACC = [ar.f32(A_XST, D2), ar.f32(A_XST + CW, D2)]
            kronTMP = [ar.f32(33 * CW, D2), ar.f32(35 * CW, D2)]
        else:
            kronACC = [ar.f32(B_WST, D2), ar.f32(B_WST + D2, D2)]
            kronTMP = [ar.f32(B_WST + 2 * D2, D2), ar.f32(B_WST + 3 * D2, D2)]
            assert 4 * D2 <= max(2 * D + D // 2, 5 * D2)

        def ws(i):
            return stw[0:1, i * C:(i + 1) * C]

        sch.op("vector", lambda h: h.memset(onesC[:], 1.0), inc=semV)
        sch.op("vector", lambda h: h.memset(onesB[:], 1.0), inc=semV)
        sch.op("vector", lambda h: h.memset(onesF[:], 1.0), inc=semV)
        sch.op("vector", lambda h: h.memset(epsT[:], EPS), inc=semV)
        sch.op("vector", lambda h: h.memset(eps2T[:], EPS * EPS), inc=semV)
        sch.op("vector", lambda h: h.memset(denS[:], 0.0), inc=semV)
        sch.op("vector", lambda h: h.memset(pkS[:], 0.0), inc=semV)
        V_const = sch.val(semV)

        def dma(eng, dst, src, sem, waits=()):
            return sch.op(eng, lambda h, d=dst, s=src: h.dma_start(out=d, in_=s),
                          waits=waits, inc=sem, inc_n=16)

        lw0 = dma("sync", selt[:, :], selv[:, :], semLV)
        dma("sync", gammaT[:, :], gammaC[:, :], semLV)
        dma("sync", betaT[:, :], betaC[:, :], semLV)
        dma("sync", bkT[:, :], bkC[:, :], semLV)
        dma("sync", bkaT[:, :], bkaC[:, :], semLV)
        dma("sync", bvT[:, :], bvC[:, :], semLV)
        lw_vecs = dma("sync", cfT[:, :], cfC[:, :], semLV)
        lw0 = lw_vecs
        semLX = [semLXa, semLXb, semLXc, semLXd]
        LXc = {}

        def emit_xload(gi, waits=()):
            j, t = divmod(gi, KT)
            LXc[(j, t)] = sch.op("scalar", lambda h, j_=j, t_=t, g_=gi: h.dma_start(
                out=xstage[g_ % 4][:, :], in_=xT[128 * t_:128 * (t_ + 1), C * j_:C * (j_ + 1)]),
                waits=waits, inc=semLX[gi % 4], inc_n=16)

        for gi in range(min(4, NCH * KT)):
            emit_xload(gi)

        S_beta = sch.op("scalar", lambda h: h.copy(out=betaB[:], in_=betaT[:]),
                        waits=[(semLV, lw_vecs)], inc=semS)
        S_bka32 = sch.op("scalar", lambda h: h.mul(out=bka32T[:], in_=bkaT[:], mul=EXP_SCALE),
                         waits=[(semLV, lw_vecs)], inc=semS)

        semLW = [semLWa, semLWb]

        def prep_matrix(matT, Wdst, stage, rawB, fold_gamma, want_cb, cb_out, bias_tile,
                        want_v, vB_out, stage_guard=None, fold_gate=None, fold_eng="vector"):
            fold_done = []
            last_ckmm = None
            for k in range(KT):
                waits = list(stage_guard) if stage_guard else []
                if len(fold_done) >= 2:
                    waits.append((semS if fold_eng == "scalar" else semV, fold_done[-2]))
                if last_ckmm is not None:
                    waits.append((semT, last_ckmm))
                ld = sch.op("sync", lambda h, k_=k, w=matT: h.dma_start(
                    out=stage[k_ % 2][:, :], in_=w[128 * k_:128 * (k_ + 1), :]),
                    waits=waits, inc=semLW[k % 2], inc_n=16)
                if want_cb:
                    for m in range(MT):
                        inc = semT if m == MT - 1 else None
                        last_ckmm = sch.op("tensor", lambda h, m_=m, k_=k: h.matmul(
                            ps_misc[:, k_ * MT + m_:k_ * MT + m_ + 1],
                            stage[k_ % 2][:, 128 * m_:128 * (m_ + 1)],
                            betaT[:, k_:k_ + 1], start=True, stop=True),
                            waits=[(semLW[k % 2], ld), (semLV, lw_vecs)], inc=inc) or last_ckmm
                fg = list(fold_gate) if fold_gate else []
                fsem = semS if fold_eng == "scalar" else semV
                if fold_gamma:
                    if fold_eng == "scalar":
                        fv = sch.op("scalar", lambda h, k_=k: h.activation(
                            out=Wdst(k_)[:, :], in_=stage[k_ % 2][:, :], func=AF.Identity,
                            scale=gammaT[:, k_:k_ + 1]),
                            waits=[(semLW[k % 2], ld), (semLV, lw_vecs)] + fg, inc=fsem)
                    else:
                        fv = sch.op("vector", lambda h, k_=k: h.tensor_scalar(
                            Wdst(k_)[:, :], stage[k_ % 2][:, :], gammaT[:, k_:k_ + 1], None, ALU.mult),
                            waits=[(semLW[k % 2], ld), (semLV, lw_vecs)] + fg, inc=fsem)
                else:
                    fv = sch.op("vector", lambda h, k_=k: h.tensor_copy(
                        Wdst(k_)[:, :], stage[k_ % 2][:, :]),
                        waits=[(semLW[k % 2], ld)] + fg, inc=semV)
                fold_done.append(fv)
            ck_psum = sch.val(semT)
            cb_v = None
            if want_cb:
                cb_v = sch.op("vector", lambda h: h.tensor_add(cb_out[:, :], ps_misc[:, 0:MT], bias_tile[:, :]),
                              waits=[(semT, ck_psum), (semLV, lw_vecs)], inc=semV)
                for k in range(1, KT):
                    cb_v = sch.op("vector", lambda h, k_=k: h.tensor_add(
                        cb_out[:, :], cb_out[:, :], ps_misc[:, k_ * MT:(k_ + 1) * MT]),
                        waits=[(semV, cb_v)], inc=semV)
            vlast = None
            if want_v:
                nhalf = (D + PSW - 1) // PSW
                for hh in range(nhalf):
                    w = min(PSW, D - PSW * hh)
                    for k in range(KT):
                        waits = [(semV, V_const),
                                 (semS if fold_eng == "scalar" else semV, fold_done[k])]
                        if vlast is not None:
                            waits.append((semS, vlast))
                        elif want_cb:
                            waits.append((semV, cb_v))
                        sch.op("tensor", lambda h, k_=k, hh_=hh, w_=w: h.matmul(
                            ps_misc[0:1, 0:w_], onesB[:, :], Wdst(k_)[:, PSW * hh_:PSW * hh_ + w_],
                            start=(k_ == 0), stop=(k_ == KT - 1)),
                            waits=waits, inc=semT if k == KT - 1 else None)
                    tps = sch.val(semT)
                    vlast = sch.op("scalar", lambda h, hh_=hh, w_=w: h.copy(
                        out=vB_out[0:1, PSW * hh_:PSW * hh_ + w_], in_=ps_misc[0:1, 0:w_]),
                        waits=[(semT, tps)], inc=semS)
            return fold_done, cb_v, vlast

        wk_fold, ck_v, vk_s = prep_matrix(wkT, W0, wstageA, rawB_A, True, True, ckT, bkT, True, vkB)
        WK_END = [(semV, sch.val(semV)), (semS, sch.val(semS)), (semT, sch.val(semT))]
        wka_fold, _, _ = prep_matrix(wkaT, W1, wstageA, rawB_A, False, False, None, None, False, None,
                                     stage_guard=WK_END)
        for half, (ag, bg) in enumerate(((atgR, btgR), (atgI, btgI))):
            for c in range(KC):
                for t in range(T2):
                    off = ((half * KC + c) * T2 + t) * BB
                    dma("sync", atg_s[:, off:off + BB], ag[c, 128 * t:128 * (t + 1), :], semKL)
                boff = (half * KC + c) * 16
                dma("sync", btg_s[:, boff:boff + 16], bg[c, :, :], semKL)
        KRON_LOADED = sch.val(semKL)
        WK_READY = [(semV, wk_fold[k]) for k in range(KT)]
        WKA_READY = [(semV, wka_fold[k]) for k in range(KT)]
        VK_READY = (semS, vk_s)
        CK_READY = (semV, ck_v)
        A_SPACE_GATE = [(semV, sch.val(semV)), (semS, sch.val(semS)), (semT, sch.val(semT))]

        XBc = {}; ke_evac = {}; KEYv = {}; KKv = {}; W0v = {}
        SPK = {}; SPD = {}; DENSv = {}; kw_s = {}
        sbc_ev = {}; stats_done = {}; sq = {}
        SPA_done = {}
        stats_mfree = {}
        SROWV = {}; SROWS = {}
        kaps_last = {}

        def emit_casts(j):
            for t in range(KT):
                gi0 = j * KT + t
                XBc[(j, t)] = sch.op("scalar", lambda h, j_=j, t_=t, g_=gi0: h.copy(
                    out=xB(t_)[:, C * j_:C * (j_ + 1)], in_=xstage[g_ % 4][:, :]),
                    waits=[(semLX[gi0 % 4], LXc[(j, t)])], inc=semS)
                gi = gi0 + 4
                if gi < NCH * KT:
                    emit_xload(gi, waits=[(semS, XBc[(j, t)])])

        def emit_stats(j):
            for t in range(KT):
                gi = j * KT + t
                waits = [(semS, XBc[(j, t)])]
                if gi >= 2:
                    waits.append((semT, sq[("mm", gi - 2)]))
                sq[(j, t)] = sch.op("scalar", lambda h, j_=j, t_=t: h.activation(
                    out=xsqB[t_ % 2][:, :], in_=xB(t_)[:, C * j_:C * (j_ + 1)], func=AF.Square),
                    waits=waits, inc=semS)
                sch.op("tensor", lambda h, j_=j, t_=t: h.matmul(
                    ps_sx[0:1, :], onesB[:, :], xB(t_)[:, C * j_:C * (j_ + 1)],
                    start=(t_ == 0), stop=(t_ == KT - 1)),
                    waits=[(semS, XBc[(j, t)]), (semV, V_const)] +
                          ([(semV, stats_mfree[j - 1])] if j > 0 else []))
                sq[("mm", gi)] = sch.op("tensor", lambda h, j_=j, t_=t: h.matmul(
                    ps_sxx[0:1, :], onesB[:, :], xsqB[t_ % 2][:, :],
                    start=(t_ == 0), stop=(t_ == KT - 1)),
                    waits=[(semS, sq[(j, t)])], inc=semT)
            STATS_PS = sch.val(semT)
            v_m = sch.op("vector", lambda h: h.tensor_scalar_mul(ws(0), ps_sx[0:1, :], RD),
                         waits=[(semT, STATS_PS)] +
                               ([(semV, stats_done[j - 1])] if j > 0 else []),
                         inc=semV)
            v_msq = sch.op("vector", lambda h: h.tensor_scalar_mul(ws(1), ps_sxx[0:1, :], RD),
                           waits=[(semT, STATS_PS)] +
                                 ([(semV, stats_done[j - 1])] if j > 0 else []),
                           inc=semV)
            stats_mfree[j] = v_msq
            v_m2 = sch.op("vector", lambda h: h.tensor_mul(ws(2), ws(0), ws(0)),
                          waits=[(semV, v_m)], inc=semV)
            v_var = sch.op("vector", lambda h: h.tensor_sub(ws(3), ws(1), ws(2)),
                           waits=[(semV, v_msq), (semV, v_m2)], inc=semV)
            s_sq1 = sch.op("scalar", lambda h: h.activation(out=ws(4), in_=ws(3), func=AF.Sqrt, bias=epsT[0:1, 0:1]),
                           waits=[(semV, v_var)], inc=semS)
            s_sqS = sch.op("scalar", lambda h: h.activation(out=ws(1), in_=ws(3), func=AF.Sqrt,
                                                            bias=eps2T[0:1, 0:1], scale=1.0 + EPS),
                           waits=[(semV, v_var)], inc=semS)
            v_s1 = sch.op("vector", lambda h, j_=j: h.reciprocal(
                out=s1_row[0:1, C * j_:C * (j_ + 1)], in_=ws(4)),
                waits=[(semS, s_sq1)], inc=semV)
            v_s = sch.op("vector", lambda h, j_=j: h.reciprocal(
                out=s_row[0:1, C * j_:C * (j_ + 1)], in_=ws(1)),
                waits=[(semS, s_sqS)], inc=semV)
            v_sm = sch.op("vector", lambda h, j_=j: h.tensor_mul(
                ws(2), s_row[0:1, C * j_:C * (j_ + 1)], ws(0)),
                waits=[(semV, v_s)], inc=semV)
            v_smB = sch.op("vector", lambda h, j_=j: h.tensor_scalar(
                smB_row[0:1, C * j_:C * (j_ + 1)], ws(2), -1.0, None, ALU.mult),
                waits=[(semV, v_sm)], inc=semV)
            v_s1m = sch.op("vector", lambda h, j_=j: h.tensor_mul(
                ws(4), s1_row[0:1, C * j_:C * (j_ + 1)], ws(0)),
                waits=[(semV, v_s1), (semV, v_sm)], inc=semV)
            v_s1mB = sch.op("vector", lambda h, j_=j: h.tensor_scalar(
                s1mB_row[0:1, C * j_:C * (j_ + 1)], ws(4), -1.0, None, ALU.mult),
                waits=[(semV, v_s1m)], inc=semV)
            stats_done[j] = v_s1mB
            SROWV[j] = v_s
            SROWS[j] = v_smB

        emit_casts(0)
        if NCH > 1:
            emit_casts(1)
        emit_stats(0)

        RLs = {}
        semSPD = [semSPDa, semSPDb]

        def emit_scan_tile(j, t):
            waits = [(semS, kw_s[(j, t)]), (semV, V_const)]
            if j > 0:
                waits += [(semV, DENSv[(j - 1, t)]), (semV, W0v[(j - 1, t)]),
                          (semSPD[t % 2], SPD[(j - 1, t)]), (semSPK, SPA_done[j - 1][1])]
            sc1 = sch.op("vector", lambda h, t_=t: h.tensor_tensor_scan(
                denomL[t_][:, :], onesC[:, :], kwkk[t_][:, :], denS[:, t_:t_ + 1],
                ALU.mult, ALU.add),
                waits=waits, inc=semV)
            DENSv[(j, t)] = sch.op("vector", lambda h, t_=t: h.tensor_copy(
                denS[:, t_:t_ + 1], denomL[t_][:, C - 1:C]),
                waits=[(semV, sc1)], inc=semV)
            waits = [(semV, sc1)]
            prevrl = RLs.get((j, t - 2), RLs.get((j - 1, MT - 2 + t)))
            if prevrl:
                waits.append((semS, prevrl))
            SPD[(j, t)] = sch.op("gpsimd", lambda h, j_=j, t_=t: h.dma_start(
                out=sc_den[j_, 128 * t_:128 * (t_ + 1), :], in_=denomL[t_][:, :]),
                waits=waits, inc=semSPD[t % 2], inc_n=16)
            KKv[(j, t)] = sch.op("vector", lambda h, t_=t: h.scalar_tensor_tensor(
                kwkk[t_][:, :], keysT[t_][:, :], ckT[:, t_:t_ + 1], kwkk[t_][:, :],
                ALU.add, ALU.mult),
                waits=[(semV, sc1), (semS, KEYv[(j, t)]), CK_READY], inc=semV)
            rl0 = sch.op("scalar", lambda h, t_=t: h.activation(
                out=denomL[t_][:, :], in_=denomL[t_][:, :], func=AF.Ln),
                waits=[(semSPD[t % 2], SPD[(j, t)]), (semV, DENSv[(j, t)])], inc=semS)
            RLs[(j, t)] = sch.op("scalar", lambda h, t_=t: h.activation(
                out=denomL[t_][:, :], in_=denomL[t_][:, :], func=AF.Exp, scale=-1.0),
                waits=[(semS, rl0)], inc=semS)
            SPK[(j, t)] = sch.op("gpsimd", lambda h, j_=j, t_=t: h.dma_start(
                out=sc_kk[j_, 128 * t_:128 * (t_ + 1), :], in_=kwkk[t_][:, :]),
                waits=[(semV, KKv[(j, t)])], inc=semSPK, inc_n=16)

        def emit_w0(j, t):
            waits = [(semV, KKv[(j, t)]), (semS, RLs[(j, t)])]
            prev = W0v.get((j, t - 2), W0v.get((j - 1, MT - 2 + t)))
            if prev:
                waits.append((semV, prev))
            W0v[(j, t)] = sch.op("vector", lambda h, j_=j, t_=t: h.scalar_tensor_tensor(
                w0tmp[t_ % 2][:, :], kwkk[t_][:, :], 1.0, denomL[t_][:, :],
                ALU.mult, ALU.mult,
                accum_out=pk0p[:, (j_ * MT + t_):(j_ * MT + t_ + 1)]),
                waits=waits, inc=semV)

        for j in range(NCH):
            t_bc = sch.op("tensor", lambda h, j_=j: h.matmul(
                ps_bc[:, :], onesF[0:1, :], s_row[0:1, C * j_:C * (j_ + 1)], start=True, stop=True),
                waits=[(semV, SROWV[j]), (semV, V_const)] +
                      ([(semS, sbc_ev[j - 1])] if j > 0 else []),
                inc=semT)
            sbc_ev[j] = sch.op("scalar", lambda h: h.copy(out=sbcA[:, :], in_=ps_bc[:, :]),
                               waits=[(semT, t_bc)] +
                                     ([(semV, ke_evac[(j - 1, MT - 1)])] if j > 0 else
                                      [(semV, wka_fold[-1])]),
                               inc=semS)

            for t in range(MT):
                if j > 0:
                    emit_scan_tile(j - 1, t)
                    if t >= 2:
                        emit_w0(j - 1, t - 2)
                m = t
                bank = pool[m % 2]
                for k in range(KT):
                    waits = [WK_READY[k], (semS, XBc[(j, k)])]
                    if k == 0:
                        prev = ke_evac.get((j, m - 2), ke_evac.get((j - 1, MT - 2 + m)))
                        if prev:
                            waits.append((semV, prev))
                    sch.op("tensor", lambda h, j_=j, m_=m, k_=k, b=bank: h.matmul(
                        b[:, :], W0(k_)[:, 128 * m_:128 * (m_ + 1)], xB(k_)[:, C * j_:C * (j_ + 1)],
                        start=(k_ == 0), stop=False),
                        waits=waits)
                KPS = sch.op("tensor", lambda h, j_=j, m_=m, b=bank: h.matmul(
                    b[:, :], vkB[0:1, 128 * m_:128 * (m_ + 1)], smB_row[0:1, C * j_:C * (j_ + 1)],
                    start=False, stop=True),
                    waits=[VK_READY, (semV, SROWS[j])], inc=semT)
                waits = [(semT, KPS), (semS, sbc_ev[j])]
                if j == 0:
                    waits += A_SPACE_GATE
                else:
                    waits.append((semV, KKv[(j - 1, m)]))
                ke_evac[(j, m)] = sch.op("vector", lambda h, m_=m, b=bank: h.tensor_mul(
                    keysT[m_][:, :], b[:, :], sbcA[:, :]),
                    waits=waits, inc=semV)
                waits = [(semV, ke_evac[(j, m)]), CK_READY]
                if j > 0:
                    waits.append((semT, kaps_last[j - 1]))
                KEYv[(j, m)] = sch.op("scalar", lambda h, m_=m: h.activation(
                    out=keysB[m_][:, :], in_=keysT[m_][:, :], func=AF.Identity, bias=ckT[:, m_:m_ + 1]),
                    waits=waits, inc=semS)
            if j > 0:
                emit_w0(j - 1, MT - 2)
                emit_w0(j - 1, MT - 1)
            if j == NCH - 1:
                KEYS_A_DONE = sch.val(semT)
            if j + 1 < NCH:
                emit_stats(j + 1)
            kw_free = sch.val(semSPK)
            kaps = {}
            for f in range(MT):
                bank = pool[2 + f % 2]
                for k in range(KT):
                    waits = [WKA_READY[k], (semS, KEYv[(j, k)])]
                    if k == 0:
                        prev = kw_s.get((j, f - 2), kw_s.get((j - 1, MT - 2 + f)))
                        if prev:
                            waits.append((semS, prev))
                    sch.op("tensor", lambda h, f_=f, k_=k, b=bank: h.matmul(
                        b[:, :], W1(k_)[:, 128 * f_:128 * (f_ + 1)], keysB[k_][:, :],
                        start=(k_ == 0), stop=(k_ == KT - 1)),
                        waits=waits, inc=(semT if k == KT - 1 else None))
                kaps[f] = sch.val(semT)
                waits = [(semT, kaps[f]), (semS, S_bka32)]
                if j > 0:
                    waits += [(semSPK, kw_free), (semV, W0v[(j - 1, f)])]
                kw_s[(j, f)] = sch.op("scalar", lambda h, f_=f: h.activation(
                    out=kwkk[f_][:, :], in_=pool[2 + f_ % 2][:, :], func=AF.Exp,
                    bias=bka32T[:, f_:f_ + 1], scale=EXP_SCALE),
                    waits=waits, inc=semS)
            kaps_last[j] = kaps[MT - 1]
            if j + 2 < NCH:
                emit_casts(j + 2)
            if j > 0:
                SPA_done[j - 1] = ((sch.val(semSPDa), sch.val(semSPDb)), sch.val(semSPK))

        for t in range(MT):
            emit_scan_tile(NCH - 1, t)
            if t >= 2:
                emit_w0(NCH - 1, t - 2)
        emit_w0(NCH - 1, MT - 2)
        emit_w0(NCH - 1, MT - 1)
        SPA_done[NCH - 1] = ((sch.val(semSPDa), sch.val(semSPDb)), sch.val(semSPK))

        vsum = None
        for j in range(NCH):
            waits = [(semV, W0v[(NCH - 1, MT - 1)])]
            if vsum:
                waits.append((semV, vsum))
            if j == 0:
                vsum = sch.op("vector", lambda h: h.tensor_copy(pk0tot[:, :], pk0p[:, 0:MT]),
                              waits=waits, inc=semV)
            else:
                vsum = sch.op("vector", lambda h, j_=j: h.tensor_add(
                    pk0tot[:, :], pk0tot[:, :], pk0p[:, j_ * MT:(j_ + 1) * MT]),
                    waits=waits, inc=semV)
        c1 = sch.op("vector", lambda h: h.tensor_scalar(
            contrib[:, 0:MT], denS[:, :], selt[:, 0:1], None, ALU.mult),
            waits=[(semV, DENSv[(NCH - 1, MT - 1)]), (semLV, lw0)], inc=semV)
        c2 = sch.op("vector", lambda h: h.tensor_scalar(
            contrib[:, MT:2 * MT], pk0tot[:, :], selt[:, 0:1], None, ALU.mult),
            waits=[(semV, vsum), (semLV, lw0)], inc=semV)
        g1 = sch.op("gpsimd", lambda h: h.dma_start(out=cc_in.ap(), in_=contrib[:, :]),
                    waits=[(semV, c1), (semV, c2)], inc=semCCD, inc_n=16)
        gcc = sch.op("gpsimd", lambda h: h.collective_compute(
            "AllReduce", ALU.add, replica_groups=cfg["pairs"],
            ins=[cc_in.ap().opt()], outs=[cc_out.ap().opt()]),
            waits=[(semCCD, g1)], inc=semCC)
        g2 = sch.op("gpsimd", lambda h: h.dma_start(out=carryraw[:, :], in_=cc_out.ap()),
                    waits=[(semCC, gcc)], inc=semCCD, inc_n=16)
        vcD = sch.op("vector", lambda h: h.tensor_scalar(
            cD[:, :], carryraw[:, 0:MT], selt[:, 1:2], None, ALU.mult),
            waits=[(semCCD, g2)], inc=semV)
        vcP = sch.op("vector", lambda h: h.tensor_scalar(
            cP[:, :], carryraw[:, MT:2 * MT], selt[:, 1:2], None, ALU.mult),
            waits=[(semCCD, g2)], inc=semV)

        A_done_V = sch.val(semV)
        A_done_S = sch.val(semS)
        A_done_T = sch.val(semT)
        A_done_SPDa = sch.val(semSPDa)
        A_done_SPDb = sch.val(semSPDb)
        A_done_SPK = sch.val(semSPK)
        A_GATE = [(semV, A_done_V), (semS, A_done_S), (semT, A_done_T), (semSPDa, A_done_SPDa), (semSPDb, A_done_SPDb), (semSPK, A_done_SPK)]

        wv_fold, cv_v, vv_s = prep_matrix(
            wvT, W0, wstageB, rawB_B, True, True, cvT, bvT, True, vvB,
            stage_guard=[(semS, vk_s), (semV, wka_fold[-1])], fold_gate=[(semT, KEYS_A_DONE)],
            fold_eng="scalar")
        WV_READY = [(semS, wv_fold[k]) for k in range(KT)]
        VV_READY = (semS, vv_s)
        CV_READY = (semV, cv_v)

        XSTAGE_FREE = (semS, XBc[(NCH - 1, KT - 1)])
        kron_gate = [(semKL, KRON_LOADED), XSTAGE_FREE]
        if not KRON_IN_XSTAGE:
            kron_gate += [(semS, vv_s), (semV, cv_v), (semT, A_done_T)]
        group_cast = {}
        gidx = 0
        for t in range(T2):
            for half in range(2):
                p = gidx % 2
                keng = "vector" if p == 0 else "gpsimd"
                ksem = semV if p == 0 else semG
                acc = kronACC[p]
                tmp = kronTMP[p]
                prev_mul = None
                for c in range(KC):
                    off = ((half * KC + c) * T2 + t) * BB
                    boff = (half * KC + c) * 16
                    a_bc = atg_s[:, off:off + BB].unsqueeze(2).broadcast_to([128, BB, 16])
                    b_bc = btg_s[:, boff:boff + 16].unsqueeze(1).broadcast_to([128, BB, 16])
                    if c == 0:
                        waits = list(kron_gate)
                        if gidx - 2 in group_cast:
                            waits.append((semS, group_cast[gidx - 2]))
                        prev_mul = sch.op(keng, lambda h, a_=a_bc, b_=b_bc, o_=acc: h.tensor_tensor(
                            o_[:, :].rearrange("p (i e) -> p i e", e=16), a_, b_, ALU.mult),
                            waits=waits, inc=ksem)
                    else:
                        vm = sch.op(keng, lambda h, a_=a_bc, b_=b_bc, o_=tmp: h.tensor_tensor(
                            o_[:, :].rearrange("p (i e) -> p i e", e=16), a_, b_, ALU.mult),
                            waits=list(kron_gate) + [(ksem, prev_mul)], inc=ksem)
                        prev_mul = sch.op(keng, lambda h, o_=acc, i_=tmp: h.tensor_add(
                            o_[:, :], o_[:, :], i_[:, :]),
                            waits=[(ksem, vm)], inc=ksem)
                if half == 0:
                    casts = [(t, 0, 1.0), (T2 + t, D2, 1.0)]
                else:
                    casts = [(t, D2, 1.0), (T2 + t, 0, -1.0)]
                for d_, o_, s_ in casts:
                    prev_acc = sch.op("scalar", lambda h, dd=d_, oo=o_, ss=s_, a_=acc: h.mul(
                        out=W1(dd)[:, oo:oo + D2], in_=a_[:, :], mul=ss),
                        waits=[(ksem, prev_mul)] +
                              ([(semT, A_done_T)] if (t == 0 and half == 0) else []),
                        inc=semS)
                group_cast[gidx] = prev_acc
                gidx += 1
        WFFN_ALL = (semS, prev_acc)
        KRON_V_DONE = (semV, sch.val(semV))
        KRON_G_DONE = (semG, sch.val(semG))
        vlast = None
        nhalf = (D + PSW - 1) // PSW
        for hh in range(nhalf):
            w = min(PSW, D - PSW * hh)
            for k in range(KT):
                waits = [WFFN_ALL, (semV, V_const)]
                if vlast is not None:
                    waits.append((semS, vlast))
                else:
                    waits.append((semS, vv_s))
                sch.op("tensor", lambda h, k_=k, hh_=hh, w_=w: h.matmul(
                    ps_misc[0:1, 0:w_], onesB[:, :], W1(k_)[:, PSW * hh_:PSW * hh_ + w_],
                    start=(k_ == 0), stop=(k_ == KT - 1)),
                    waits=waits, inc=semT if k == KT - 1 else None)
            tps = sch.val(semT)
            vlast = sch.op("scalar", lambda h, hh_=hh, w_=w: h.copy(
                out=vfB[0:1, PSW * hh_:PSW * hh_ + w_], in_=ps_misc[0:1, 0:w_]),
                waits=[(semT, tps)], inc=semS)
        VF_READY = (semS, vlast)

        RK = {}; RDn = {}; RX = {}; WBv = {}; SC2v = {}; PKSv = {}
        RK_done = {}; RD_done = {}; semRX = [semRXa, semRXb]
        AS1 = {}; AS2 = {}; AS3 = {}; VEv = {}; FEv = {}; VBs = {}; FBs = {}
        OUTg = {}; OUT_done = {}; sbc2_ev = {}; s1bc_ev = {}; FPSd = {}

        def emit_tail(j, t):
            AS2[(j, t)] = sch.op("vector", lambda h, t_=t: h.tensor_add(
                pkbuf[t_][:, :], pkbuf[t_][:, :], denR[t_][:, :]),
                waits=[(semV, AS1[(j, t)]), (semS, FBs[(j, t)])], inc=semV)
            AS3[(j, t)] = sch.op("gpsimd", lambda h, t_=t: h.tensor_add(
                pkbuf[t_][:, :], pkbuf[t_][:, :], xres[t_ % 2][:, :]),
                waits=[(semV, AS2[(j, t)]), (semRX[(j * MT + t) % 2], RX[(j, t)])], inc=semG)
            OUTg[(j, t)] = sch.op("gpsimd", lambda h, j_=j, t_=t: h.dma_start(
                out=outT[128 * t_:128 * (t_ + 1), C * j_:C * (j_ + 1)], in_=pkbuf[t_][:, :]),
                waits=[(semG, AS3[(j, t)])], inc=semOUT, inc_n=16)

        for j in range(NCH):
            for t in range(MT):
                waits = list(A_GATE)
                if j > 0:
                    waits.append((semV, SC2v[(j - 1, t)]))
                RK[(j, t)] = sch.op("sync", lambda h, j_=j, t_=t: h.dma_start(
                    out=kkB[t_][:, :], in_=sc_kk[j_, 128 * t_:128 * (t_ + 1), :]),
                    waits=waits, inc=semRK, inc_n=16)
                waits = list(A_GATE)
                if j > 0:
                    waits.append((semV, AS2[(j - 1, t)]))
                RDn[(j, t)] = sch.op("sync", lambda h, j_=j, t_=t: h.dma_start(
                    out=denR[t_][:, :], in_=sc_den[j_, 128 * t_:128 * (t_ + 1), :]),
                    waits=waits, inc=semRD, inc_n=16)
            RK_done[j] = sch.val(semRK)
            RD_done[j] = sch.val(semRD)
            t_bc = sch.op("tensor", lambda h, j_=j: h.matmul(
                ps_bc[:, :], onesF[0:1, :], s_row[0:1, C * j_:C * (j_ + 1)], start=True, stop=True),
                waits=[(semT, A_done_T), (semS, A_done_S)] +
                      ([(semS, s1bc_ev[j - 1])] if j > 0 else []),
                inc=semT)
            sbc2_ev[j] = sch.op("scalar", lambda h: h.copy(out=sbcB[:, :], in_=ps_bc[:, :]),
                                waits=[(semT, t_bc), (semS, A_done_S)] +
                                      ([(semV, VEv[(j - 1, MT - 1)])] if j > 0 else []),
                                inc=semS)
            t_bc1 = sch.op("tensor", lambda h, j_=j: h.matmul(
                ps_bc[:, :], onesF[0:1, :], s1_row[0:1, C * j_:C * (j_ + 1)], start=True, stop=True),
                waits=[(semS, sbc2_ev[j])], inc=semT)
            s1bc_ev[j] = sch.op("scalar", lambda h: h.copy(out=s1bcB[:, :], in_=ps_bc[:, :]),
                                waits=[(semT, t_bc1)] +
                                      ([(semV, FEv[(j - 1, MT - 1)])] if j > 0 else
                                       [KRON_V_DONE, KRON_G_DONE]),
                                inc=semS)
            for m in range(MT):
                bank = pool[m % 2]
                for k in range(KT):
                    waits = [WV_READY[k]]
                    if k == 0:
                        prev = VEv.get((j, m - 2), VEv.get((j - 1, MT - 2 + m)))
                        if prev:
                            waits.append((semV, prev))
                        else:
                            waits.append((semV, A_done_V))
                    sch.op("tensor", lambda h, j_=j, m_=m, k_=k, b=bank: h.matmul(
                        b[:, :], W0(k_)[:, 128 * m_:128 * (m_ + 1)], xB(k_)[:, C * j_:C * (j_ + 1)],
                        start=(k_ == 0), stop=False),
                        waits=waits)
                VPS = sch.op("tensor", lambda h, j_=j, m_=m, b=bank: h.matmul(
                    b[:, :], vvB[0:1, 128 * m_:128 * (m_ + 1)], smB_row[0:1, C * j_:C * (j_ + 1)],
                    start=False, stop=True),
                    waits=[VV_READY], inc=semT)
                waits = [(semT, VPS), (semS, sbc2_ev[j]), (semT, A_done_T),
                         (semS, A_done_S), (semV, A_done_V)]
                if j > 0:
                    waits.append((semV, AS1[(j - 1, m)]))
                else:
                    waits += [KRON_V_DONE, KRON_G_DONE, WFFN_ALL]
                VEv[(j, m)] = sch.op("vector", lambda h, m_=m, b=bank: h.tensor_mul(
                    vbuf[m_][:, :], b[:, :], sbcB[:, :]),
                    waits=waits, inc=semV)
                VBs[(j, m)] = sch.op("scalar", lambda h, m_=m: h.add(
                    out=vbuf[m_][:, :], in_=vbuf[m_][:, :], add=cvT[:, m_:m_ + 1]),
                    waits=[(semV, VEv[(j, m)]), CV_READY], inc=semS)
            def emit_ffn_group(j, m):
                bank = pool[2 + m % 2]
                for k in range(KT):
                    waits = [WFFN_ALL]
                    if k == 0:
                        prev = FEv.get((j, m - 2), FEv.get((j - 1, MT - 2 + m)))
                        if prev:
                            waits.append((semV, prev))
                    sch.op("tensor", lambda h, j_=j, m_=m, k_=k, b=bank: h.matmul(
                        b[:, :], W1(k_)[:, 128 * m_:128 * (m_ + 1)], xB(k_)[:, C * j_:C * (j_ + 1)],
                        start=(k_ == 0), stop=False),
                        waits=waits)
                FPSd[(j, m)] = sch.op("tensor", lambda h, j_=j, m_=m, b=bank: h.matmul(
                    b[:, :], vfB[0:1, 128 * m_:128 * (m_ + 1)], s1mB_row[0:1, C * j_:C * (j_ + 1)],
                    start=False, stop=True),
                    waits=[VF_READY], inc=semT)

            for m in range(min(2, MT)):
                emit_ffn_group(j, m)
            DFv = {}
            for t in range(MT):
                DFv[t] = sch.op("vector", lambda h, t_=t: h.tensor_scalar(
                    denR[t_][:, :], denR[t_][:, :], cD[:, t_:t_ + 1], None, ALU.add),
                    waits=[(semRD, RD_done[j]), (semV, vcD)], inc=semV)
            RFs = {}
            for t in range(MT):
                rf0 = sch.op("scalar", lambda h, t_=t: h.activation(
                    out=denR[t_][:, :], in_=denR[t_][:, :], func=AF.Ln),
                    waits=[(semV, DFv[t])], inc=semS)
                RFs[t] = sch.op("scalar", lambda h, t_=t: h.activation(
                    out=denR[t_][:, :], in_=denR[t_][:, :], func=AF.Exp, scale=-1.0),
                    waits=[(semS, rf0)], inc=semS)

            for t in range(MT):
                waits = list(A_GATE)
                gi = j * MT + t
                if gi >= 2:
                    pj, pt = divmod(gi - 2, MT)
                    waits.append((semG, AS3[(pj, pt)]))
                else:
                    waits += [KRON_V_DONE, KRON_G_DONE, WFFN_ALL]
                RX[(j, t)] = sch.op("scalar", lambda h, j_=j, t_=t: h.dma_start(
                    out=xres[t_ % 2][:, :], in_=xT[128 * t_:128 * (t_ + 1), C * j_:C * (j_ + 1)]),
                    waits=waits, inc=semRX[gi % 2], inc_n=16)
                WBv[(j, t)] = sch.op("vector", lambda h, t_=t: h.tensor_mul(
                    kkB[t_][:, :], kkB[t_][:, :], denR[t_][:, :]),
                    waits=[(semRK, RK_done[j]), (semS, RFs[t])], inc=semV)
                waits = [(semV, WBv[(j, t)]), (semV, V_const)]
                if j > 0:
                    waits += [(semV, PKSv[(j - 1, t)]), (semOUT, OUT_done[j - 1])]
                sc2 = sch.op("vector", lambda h, t_=t: h.tensor_tensor_scan(
                    pkbuf[t_][:, :], onesC[:, :], kkB[t_][:, :], pkS[:, t_:t_ + 1],
                    ALU.mult, ALU.add),
                    waits=waits, inc=semV)
                SC2v[(j, t)] = sc2
                PKSv[(j, t)] = sch.op("vector", lambda h, t_=t: h.tensor_copy(
                    pkS[:, t_:t_ + 1], pkbuf[t_][:, C - 1:C]),
                    waits=[(semV, sc2)], inc=semV)
                FEv[(j, t)] = sch.op("vector", lambda h, t_=t, b=pool[2 + t % 2]: h.tensor_mul(
                    denR[t_][:, :], b[:, :], s1bcB[:, :]),
                    waits=[(semT, FPSd[(j, t)]), (semS, s1bc_ev[j]), (semV, WBv[(j, t)])],
                    inc=semV)
                FBs[(j, t)] = sch.op("scalar", lambda h, t_=t: h.add(
                    out=denR[t_][:, :], in_=denR[t_][:, :], add=cfT[:, t_:t_ + 1]),
                    waits=[(semV, FEv[(j, t)])], inc=semS)
                if t + 2 < MT:
                    emit_ffn_group(j, t + 2)
                AS1[(j, t)] = sch.op("vector", lambda h, t_=t: h.scalar_tensor_tensor(
                    pkbuf[t_][:, :], pkbuf[t_][:, :], cP[:, t_:t_ + 1], vbuf[t_][:, :],
                    ALU.add, ALU.mult),
                    waits=[(semV, PKSv[(j, t)]), (semS, VBs[(j, t)]), (semV, vcP)],
                    inc=semV)
                if t >= 1:
                    emit_tail(j, t - 1)
            emit_tail(j, MT - 1)
            OUT_done[j] = sch.val(semOUT)

        final_out = sch.val(semOUT)
        sch.raw("gpsimd", lambda h: h.wait_ge(semOUT, final_out))

        with nc.Block() as block:
            @block.sync
            def _(eng):
                for f in sch.ops["sync"]:
                    f(eng)

            @block.scalar
            def _(eng):
                for f in sch.ops["scalar"]:
                    f(eng)

            @block.vector
            def _(eng):
                for f in sch.ops["vector"]:
                    f(eng)

            @block.tensor
            def _(eng):
                for f in sch.ops["tensor"]:
                    f(eng)

            @block.gpsimd
            def _(eng):
                for f in sch.ops["gpsimd"]:
                    f(eng)

    return nc



def _col_tile(v, KT):
    return np.ascontiguousarray(np.asarray(v, np.float32).reshape(KT, 128).T)


def prep_inputs(cfg, x, Wk, bk, Wka, bka, Wv, bv, gamma, beta,
                G1_R, G2_R, G1_I, G2_I, bias_real, bias_imag):
    D = cfg["D"]; C = cfg["C"]; NCH = cfg["NCH"]; n_cores = cfg["n_cores"]
    D2 = cfg["D2"]; KC = cfg["KC"]
    KT = D // 128
    TCORE = C * NCH
    gg, ee = 16, 16

    xf = np.asarray(x, np.float32).reshape(-1, D)
    assert xf.shape[0] == n_cores * TCORE

    def expand(G1, G2):
        KA = np.ascontiguousarray(np.asarray(G1, np.float32)[0].transpose(1, 2, 0))
        KB = np.ascontiguousarray(np.asarray(G2, np.float32)[:, :, 0, :].transpose(0, 2, 1))
        at = np.stack([np.repeat(KA[c], gg, axis=0) for c in range(KC)])
        bt = np.stack([np.tile(KB[c], (128 // gg, 1)) for c in range(KC)])
        return np.ascontiguousarray(at), np.ascontiguousarray(bt)

    atR, btR = expand(G1_R, G2_R)
    atI, btI = expand(G1_I, G2_I)
    cf = np.concatenate([np.asarray(bias_real, np.float32), np.asarray(bias_imag, np.float32)])

    shared = dict(
        wkT=np.ascontiguousarray(np.asarray(Wk, np.float32).T),
        wkaT=np.ascontiguousarray(np.asarray(Wka, np.float32).T),
        wvT=np.ascontiguousarray(np.asarray(Wv, np.float32).T),
        gammaC=_col_tile(gamma, KT), betaC=_col_tile(beta, KT),
        bkC=_col_tile(bk, KT), bkaC=_col_tile(bka, KT), bvC=_col_tile(bv, KT),
        cfC=_col_tile(cf, KT),
        atgR=atR, atgI=atI, btgR=btR, btgI=btI,
    )
    in_maps = []
    for c in range(n_cores):
        sel = np.zeros((128, 2), np.float32)
        sel[:, 0] = 1.0 if c % 2 == 0 else 0.0
        sel[:, 1] = 0.0 if c % 2 == 0 else 1.0
        m = dict(shared)
        m["selv"] = sel
        m["xT"] = np.ascontiguousarray(xf[c * TCORE:(c + 1) * TCORE].T)
        in_maps.append(m)
    return in_maps


def kernel(L0_attn_hidden_states, L0_attn_attention_mask, L0_attn_Wk, L0_attn_bk,
           L0_attn_Wka, L0_attn_bka, L0_attn_Wv, L0_attn_bv, L0_attn_gamma,
           L0_attn_beta, L0_ffn_G1_R, L0_ffn_G2_R, L0_ffn_G1_I, L0_ffn_G2_I,
           L0_ffn_bias_real, L0_ffn_bias_imag):
    cfg = FULL_CFG
    x = np.asarray(L0_attn_hidden_states, np.float32)
    B, S, D = x.shape
    in_maps = prep_inputs(cfg, x, L0_attn_Wk, L0_attn_bk, L0_attn_Wka, L0_attn_bka,
                          L0_attn_Wv, L0_attn_bv, L0_attn_gamma, L0_attn_beta,
                          L0_ffn_G1_R, L0_ffn_G2_R, L0_ffn_G1_I, L0_ffn_G2_I,
                          L0_ffn_bias_real, L0_ffn_bias_imag)
    nc = build_program(cfg)
    res = run_bass_kernel_spmd(nc, in_maps, core_ids=list(range(cfg["n_cores"])))
    parts = [np.asarray(res.results[c]["outT"]).T for c in range(cfg["n_cores"])]
    out = np.concatenate(parts, axis=0).reshape(B, S, D).astype(np.float32)
    return out
